# version 03 (final)
# baseline (speedup 1.0000x reference)
import math
import numpy as np
from contextlib import ExitStack

import ml_dtypes
import concourse.bass as bass
import concourse.mybir as mybir
from concourse.bass_utils import run_bass_kernel_spmd

F32 = mybir.dt.float32
BF16 = mybir.dt.bfloat16
AF = mybir.ActivationFunctionType
ALU = mybir.AluOpType
AX = mybir.AxisListType

NCORES = 8
BPC = 2
T = 2048
DM = 1024
BF = ml_dtypes.bfloat16


class TR:
    __slots__ = ("t", "k")
    def __init__(self, t, k):
        self.t = t
        self.k = k
    @property
    def ap(self):
        return self.t[:]


class Prog:
    ENG = ("pe", "dve", "act", "pool", "sp")

    def __init__(self, name):
        self.nc = bass.Bass("TRN2", target_bir_lowering=False)
        self.name = name
        self.es = ExitStack()
        self.seq = []
        self.count = {}
        for e in self.ENG:
            self.count["sem_" + e] = 0
        for q in ("sp", "pool", "act", "dve"):
            self.count["dma_" + q] = 0
        self.last_writer = {}
        self.readers = {}
        self.watermark = {e: {} for e in self.ENG}
        self.sems = {}
        self._names = set()
        self.in_names = []
        self.out_names = []

    def dram_in(self, name, shape, dtype=F32):
        self.in_names.append(name)
        return self.nc.dram_tensor(name, list(shape), dtype, kind="ExternalInput").ap()

    def dram_out(self, name, shape, dtype=F32):
        self.out_names.append(name)
        return self.nc.dram_tensor(name, list(shape), dtype, kind="ExternalOutput").ap()

    def sb(self, name, shape, dtype=F32):
        assert name not in self._names, name
        self._names.add(name)
        return TR(self.es.enter_context(self.nc.sbuf_tensor(name, list(shape), dtype)), name)

    def ps(self, name, shape, dtype=F32):
        assert name not in self._names, name
        self._names.add(name)
        return TR(self.es.enter_context(self.nc.psum_tensor(name, list(shape), dtype)), name)

    def _deps_for(self, eng, reads, writes):
        deps = {}
        def add(tok):
            if tok is None:
                return
            semname, val, peng = tok
            if peng == eng and eng == "pe" and semname == "sem_pe":
                return
            if val > deps.get(semname, 0):
                deps[semname] = val
        for k in reads:
            add(self.last_writer.get(k))
        for k in writes:
            add(self.last_writer.get(k))
            for tok in self.readers.get(k, {}).values():
                add(tok)
        waits = []
        wm = self.watermark[eng]
        for semname, val in deps.items():
            if wm.get(semname, 0) < val:
                waits.append((semname, val))
                wm[semname] = val
        return waits

    def _push(self, eng, emit, reads, writes, semname, amt):
        waits = self._deps_for(eng, reads, writes)
        if eng == "pool":
            prev = self.count["sem_pool"]
            if prev > 0 and self.watermark["pool"].get("sem_pool", 0) < prev:
                waits.append(("sem_pool", prev))
                self.watermark["pool"]["sem_pool"] = prev
        self.count[semname] += amt
        tok = (semname, self.count[semname], eng)
        self.seq.append((eng, emit, waits, (semname, amt)))
        for k in writes:
            self.last_writer[k] = tok
            self.readers[k] = {}
        for k in reads:
            self.readers.setdefault(k, {})[eng] = tok
        return tok

    def run(self, eng, emit_fn, r=(), w=()):
        return self._push(eng, emit_fn, list(r), list(w), "sem_" + eng, 1)

    def mm(self, fns, r=(), w=()):
        return self._push("pe", list(fns), list(r), list(w), "sem_pe", 1)

    def dma(self, q, out_ap, in_ap, r=(), w=()):
        def emit(eng):
            return eng.dma_start(out=out_ap, in_=in_ap)
        return self._push(q, emit, list(r), list(w), "dma_" + q, 16)

    def finalize(self):
        for semname in self.count:
            self.sems[semname] = self.es.enter_context(self.nc.semaphore(semname))
        block = self.es.enter_context(self.nc.Block())
        streams = {e: [] for e in self.ENG}
        for eng, emit, waits, inc in self.seq:
            streams[eng].append((emit, waits, inc))

        def make(engname):
            def body(engobj):
                cum = 0
                for emit, waits, inc in streams[engname]:
                    for semname, val in waits:
                        engobj.wait_ge(self.sems[semname], val)
                    semname, amt = inc
                    if isinstance(emit, list):
                        insts = [f(engobj) for f in emit]
                        insts[-1].then_inc(self.sems[semname], amt)
                    else:
                        emit(engobj).then_inc(self.sems[semname], amt)
                    if semname.startswith("dma_"):
                        cum += amt
                        engobj.wait_ge(self.sems[semname], cum)
                if engname == "sp":
                    for semname in sorted(self.count):
                        if self.count[semname] > 0:
                            engobj.wait_ge(self.sems[semname], self.count[semname])
            return body

        block.tensor(make("pe"))
        block.vector(make("dve"))
        block.scalar(make("act"))
        block.gpsimd(make("pool"))
        block.sync(make("sp"))
        self.es.close()
        return self.nc


def tiles_of(total, size=128):
    out = []
    o = 0
    while o < total:
        out.append((o, min(size, total - o)))
        o += size
    return out


class Consts:
    def __init__(self, p, cap=32):
        self.cap = cap
        self.vals = []
        self.tile = p.sb("consts", [128, cap], F32)
        p.dma("sp", self.tile.ap, p.dram_in("consts_in", [128, cap], F32), w=["consts"])

    def __getitem__(self, v):
        v = float(v)
        if v not in self.vals:
            assert len(self.vals) < self.cap
            self.vals.append(v)
        return self.tile.t[:, self.vals.index(v):self.vals.index(v) + 1]

    def pn(self, v, n):
        v = float(v)
        if v not in self.vals:
            assert len(self.vals) < self.cap
            self.vals.append(v)
        return self.tile.t[0:n, self.vals.index(v):self.vals.index(v) + 1]

    def p1(self, v):
        v = float(v)
        if v not in self.vals:
            assert len(self.vals) < self.cap
            self.vals.append(v)
        return self.tile.t[0:1, self.vals.index(v):self.vals.index(v) + 1]

    def array(self):
        arr = np.zeros((128, self.cap), np.float32)
        arr[:, :len(self.vals)] = np.array(self.vals, np.float32)[None, :]
        return arr


class WStream:

    def __init__(self, p, name, dram_ap, K, M, nbuf=2):
        self.p = p
        self.name = name
        self.ap = dram_ap
        self.K = K
        self.M = M
        self.ktiles = tiles_of(K)
        self.nbuf = nbuf
        w = min(128, M)
        self.stage = [p.sb(f"{name}_st{i}", [128, w], F32) for i in range(nbuf)]
        self.wb = [[p.sb(f"{name}_wb{i}_{k}", [128, w], BF16) for k in range(len(self.ktiles))]
                   for i in range(nbuf)]
        self.psum = None

    def load(self, m0, mw, buf, kbase=0):
        p = self.p
        b = buf % self.nbuf
        stg = self.stage[b]
        for ki, (k0, kw) in enumerate(self.ktiles):
            p.dma("sp", stg.t[0:kw, 0:mw], self.ap[kbase + k0:kbase + k0 + kw, m0:m0 + mw], w=[stg.k])
            dst = self.wb[b][ki]
            p.run("pool", lambda e, d=dst, s=stg, kw=kw, mw=mw: e.tensor_copy(out=d.t[0:kw, 0:mw], in_=s.t[0:kw, 0:mw]),
                  r=[stg.k], w=[dst.k])

    def tile(self, buf, ki):
        return self.wb[buf % self.nbuf][ki]


def fm_matmul(p, ws, rhs, Ttot, out_cb, pspool, mtiles=None, nchunk=512, kbase=0):
    if mtiles is None:
        mtiles = tiles_of(ws.M)
    kt = ws.ktiles
    nch = (Ttot + nchunk - 1) // nchunk
    ws.load(mtiles[0][0], mtiles[0][1], 0, kbase)
    for mi, (m0, mw) in enumerate(mtiles):
        if mi + 1 < len(mtiles) and ws.nbuf > 1:
            ws.load(mtiles[mi + 1][0], mtiles[mi + 1][1], mi + 1, kbase)
        for ci in range(nch):
            c0 = ci * nchunk
            cw = min(nchunk, Ttot - c0)
            psum = pspool[(mi * nch + ci) % len(pspool)]
            fns = []
            reads = []
            for ki, (k0, kw) in enumerate(kt):
                wtile = ws.tile(mi, ki)
                rt = rhs[ki]
                fns.append(lambda e, wt=wtile, rt=rt, kw=kw, mw=mw, c0=c0, cw=cw, ps=psum,
                                  st=(ki == 0), sp=(ki == len(kt) - 1):
                           e.matmul(ps.t[0:mw, 0:cw], wt.t[0:kw, 0:mw], rt.t[0:kw, c0:c0 + cw],
                                    start=st, stop=sp, skip_group_check=True))
                reads.append(wtile.k)
                reads.append(rt.k)
            p.mm(fns, r=reads, w=[psum.k])
            out_cb(mi, m0, mw, c0, cw, psum)
        if mi + 1 < len(mtiles) and ws.nbuf == 1:
            ws.load(mtiles[mi + 1][0], mtiles[mi + 1][1], mi + 1, kbase)


class LNScratch:
    def __init__(self, p, name):
        self.sq = p.sb(f"{name}_sq", [128, 512], F32)
        self.ps_s = p.ps(f"{name}_pss", [1, 512], F32)
        self.ps_q = p.ps(f"{name}_psq", [1, 512], F32)
        self.var = p.sb(f"{name}_var", [1, 512], F32)
        self.nm = p.sb(f"{name}_nm", [1, 512], F32)
        self.rs = p.sb(f"{name}_rs", [1, 512], F32)
        self.bcm = p.sb(f"{name}_bcm", [128, 512], F32)
        self.bcr = p.sb(f"{name}_bcr", [128, 512], F32)
        self.xn = p.sb(f"{name}_xn", [128, 512], F32)
        self.xf = p.sb(f"{name}_xf", [128, 512], F32)


def fm_layernorm(p, C, sc, xtiles, Ttot, out_cb, ones_col_x, ones_col_f, ones_row_f, ps_b,
                 gain=None, bias=None, eps=1e-5, D_feat=None):
    nt = len(xtiles)
    if D_feat is None:
        D_feat = nt * 128
    inv_d = 1.0 / D_feat
    for c0 in range(0, Ttot, 512):
        cw = min(512, Ttot - c0)
        fns = []
        reads = [ones_col_x.k]
        for ti in range(nt):
            fns.append(lambda e, ti=ti, c0=c0, cw=cw, st=(ti == 0), sp=(ti == nt - 1):
                       e.matmul(sc.ps_s.t[0:1, 0:cw], ones_col_x.t[:, 0:1], xtiles[ti].t[:, c0:c0 + cw],
                                start=st, stop=sp, skip_group_check=True))
            reads.append(xtiles[ti].k)
        p.mm(fns, r=reads, w=[sc.ps_s.k])
        for ti in range(nt):
            p.run("act", lambda e, ti=ti, c0=c0, cw=cw: e.activation(
                out=sc.sq.t[:, 0:cw], in_=xtiles[ti].t[:, c0:c0 + cw], func=AF.Square),
                r=[xtiles[ti].k], w=[sc.sq.k])
            p.mm([lambda e, cw=cw, st=(ti == 0), sp=(ti == nt - 1):
                  e.matmul(sc.ps_q.t[0:1, 0:cw], ones_col_f.t[:, 0:1], sc.sq.t[:, 0:cw],
                           start=st, stop=sp, skip_group_check=True)],
                 r=[sc.sq.k, ones_col_f.k], w=[sc.ps_q.k])
        p.run("act", lambda e, cw=cw: e.activation(out=sc.nm.t[0:1, 0:cw], in_=sc.ps_s.t[0:1, 0:cw],
              func=AF.Copy, scale=C.p1(-inv_d)), r=[sc.ps_s.k, "consts"], w=[sc.nm.k])
        p.run("act", lambda e, cw=cw: e.activation(out=sc.var.t[0:1, 0:cw], in_=sc.ps_s.t[0:1, 0:cw],
              func=AF.Square, scale=C.p1(inv_d)), r=[sc.ps_s.k, "consts"], w=[sc.var.k])
        p.run("dve", lambda e, cw=cw: e.scalar_tensor_tensor(out=sc.var.t[0:1, 0:cw], in0=sc.ps_q.t[0:1, 0:cw],
              scalar=C.p1(inv_d), in1=sc.var.t[0:1, 0:cw], op0=ALU.mult, op1=ALU.subtract),
              r=[sc.ps_q.k, sc.var.k, "consts"], w=[sc.var.k])
        p.run("act", lambda e, cw=cw: e.activation(out=sc.var.t[0:1, 0:cw], in_=sc.var.t[0:1, 0:cw],
              func=AF.Ln, bias=C.p1(eps)), r=[sc.var.k, "consts"], w=[sc.var.k])
        p.run("act", lambda e, cw=cw: e.activation(out=sc.rs.t[0:1, 0:cw], in_=sc.var.t[0:1, 0:cw],
              func=AF.Exp, scale=C.p1(-0.5)), r=[sc.var.k, "consts"], w=[sc.rs.k])
        p.mm([lambda e, cw=cw: e.matmul(ps_b.t[:, 0:cw], ones_row_f.t[0:1, :], sc.nm.t[0:1, 0:cw], start=True, stop=True, skip_group_check=True)],
             r=[sc.nm.k, ones_row_f.k], w=[ps_b.k])
        p.run("dve", lambda e, cw=cw: e.tensor_copy(out=sc.bcm.t[:, 0:cw], in_=ps_b.t[:, 0:cw]),
              r=[ps_b.k], w=[sc.bcm.k])
        p.mm([lambda e, cw=cw: e.matmul(ps_b.t[:, 0:cw], ones_row_f.t[0:1, :], sc.rs.t[0:1, 0:cw], start=True, stop=True, skip_group_check=True)],
             r=[sc.rs.k, ones_row_f.k], w=[ps_b.k])
        p.run("dve", lambda e, cw=cw: e.tensor_copy(out=sc.bcr.t[:, 0:cw], in_=ps_b.t[:, 0:cw]),
              r=[ps_b.k], w=[sc.bcr.k])
        for ti in range(nt):
            src = xtiles[ti]
            p.run("act", lambda e, src=src, c0=c0, cw=cw: e.copy(
                out=sc.xf.t[:, 0:cw], in_=src.t[:, c0:c0 + cw]), r=[src.k], w=[sc.xf.k])
            p.run("dve", lambda e, cw=cw: e.tensor_tensor(
                out=sc.xn.t[:, 0:cw], in0=sc.xf.t[:, 0:cw], in1=sc.bcm.t[:, 0:cw], op=ALU.add),
                r=[sc.xf.k, sc.bcm.k], w=[sc.xn.k])
            p.run("dve", lambda e, cw=cw: e.tensor_tensor(
                out=sc.xn.t[:, 0:cw], in0=sc.xn.t[:, 0:cw], in1=sc.bcr.t[:, 0:cw], op=ALU.mult),
                r=[sc.xn.k, sc.bcr.k], w=[sc.xn.k])
            if gain is not None:
                p.run("dve", lambda e, ti=ti, cw=cw: e.tensor_scalar(
                    out=sc.xn.t[:, 0:cw], in0=sc.xn.t[:, 0:cw], scalar1=gain[ti], scalar2=bias[ti],
                    op0=ALU.mult, op1=ALU.add),
                    r=[sc.xn.k, "params"], w=[sc.xn.k])
            out_cb(ti, c0, cw, sc.xn.t[:, 0:cw], sc.xn.k)


def softplus(p, C, out, x, t1, t2, W):
    p.run("act", lambda e: e.activation(out=t1.t[:, 0:W], in_=x.t[:, 0:W], func=AF.Abs), r=[x.k], w=[t1.k])
    p.run("act", lambda e: e.activation(out=t1.t[:, 0:W], in_=t1.t[:, 0:W], func=AF.Exp, scale=C[-1.0]), r=[t1.k, "consts"], w=[t1.k])
    p.run("act", lambda e: e.activation(out=t1.t[:, 0:W], in_=t1.t[:, 0:W], func=AF.Ln, bias=C[1.0]), r=[t1.k, "consts"], w=[t1.k])
    p.run("act", lambda e: e.activation(out=t2.t[:, 0:W], in_=x.t[:, 0:W], func=AF.Relu), r=[x.k], w=[t2.k])
    p.run("dve", lambda e: e.tensor_tensor(out=out.t[:, 0:W], in0=t1.t[:, 0:W], in1=t2.t[:, 0:W], op=ALU.add), r=[t1.k, t2.k], w=[out.k])



D_INNER = 2048

def build_p1(dbg=False):
    p = Prog("p1")
    C = Consts(p)
    x_in = p.dram_in("x_fm", [BPC, DM, T])
    w_in = p.dram_in("W_in", [DM, 2 * D_INNER])
    xproj_B = p.dram_in("xproj_B", [D_INNER, 16])
    xproj_C = p.dram_in("xproj_C", [D_INNER, 16])
    xproj_dt = p.dram_in("xproj_dt", [D_INNER, 1])
    A_log_rep = p.dram_in("A_log_rep", [128, 256])
    w_out = p.dram_in("W_out", [D_INNER, DM])
    sel_in = p.dram_in("sel_mats", [128, 16 * 128], BF16)
    red_in = p.dram_in("red_mats", [128, 16 * 128], BF16)
    sel16_in = p.dram_in("sel16", [16, 128], BF16)
    ones_f_in = p.dram_in("ones1x128f", [1, 128], F32)
    ones_b_in = p.dram_in("ones1x128b", [1, 128], BF16)
    prm_in = p.dram_in("prm_tab", [128, 144], F32)
    x1_out = p.dram_out("x1_fm", [BPC, DM, T])
    z_scr = p.nc.dram_tensor("z_scr", [16, 128, T], BF16, kind="Internal").ap()
    if dbg:
        dbg_xn = p.dram_out("dbg_xn", [8, 128, T], BF16)
        dbg_xcs = p.dram_out("dbg_xcs", [16, 128, T], BF16)
        dbg_B = p.dram_out("dbg_B", [16, T], BF16)
        dbg_dbc0 = p.dram_out("dbg_dbc0", [1, T], BF16)
        dbg_dt = p.dram_out("dbg_dt", [16, 128, T], BF16)
        dbg_y = p.dram_out("dbg_y", [16, 128, T], BF16)
        dbg_h0 = p.dram_out("dbg_h0", [128, T], BF16)
        dbg_xc0 = p.dram_out("dbg_xc0", [128, T + 4], F32)
        dbg_cv0 = p.dram_out("dbg_cv0", [128, T], F32)

    ones_f = p.sb("ones_f", [1, 128], F32)
    ones_b = p.sb("ones_b", [1, 128], BF16)
    onec_f = p.sb("onec_f", [128, 1], F32)
    onec_b = p.sb("onec_b", [128, 1], BF16)
    p.dma("sp", ones_f.ap, ones_f_in, w=[ones_f.k])
    p.dma("sp", ones_b.ap, ones_b_in, w=[ones_b.k])
    p.dma("sp", onec_f.ap, ones_f_in.rearrange("a b -> b a"), w=[onec_f.k])
    p.dma("sp", onec_b.ap, ones_b_in.rearrange("a b -> b a"), w=[onec_b.k])
    sel_sb = p.sb("sel_sb", [128, 16 * 128], BF16)
    red_sb = p.sb("red_sb", [128, 16 * 128], BF16)
    sel16_sb = p.sb("sel16_sb", [16, 128], BF16)
    p.dma("sp", sel_sb.ap, sel_in, w=[sel_sb.k])
    p.dma("sp", red_sb.ap, red_in, w=[red_sb.k])
    p.dma("sp", sel16_sb.ap, sel16_in, w=[sel16_sb.k])
    prm = p.sb("prm", [128, 144], F32)
    p.dma("sp", prm.ap, prm_in, w=["params"])
    NW, NB, CW, CB, DTW, DTB, DV = 0, 8, 16, 80, 96, 112, 128
    def pc(base, i):
        return prm.t[:, base + i:base + i + 1]
    alog = p.sb("alog", [128, 256], F32)
    p.dma("sp", alog.ap, A_log_rep, w=[alog.k])
    napos = p.sb("napos", [128, 256], F32)
    p.run("act", lambda e: e.activation(out=napos.ap, in_=alog.ap, func=AF.Exp), r=[alog.k], w=[napos.k])
    p.run("dve", lambda e: e.tensor_scalar(out=napos.ap, in0=napos.ap, scalar1=C[-1.0], scalar2=None, op0=ALU.mult),
          r=[napos.k, "consts"], w=[napos.k])

    psP = [p.ps(f"psp{i}", [128, 512], F32) for i in range(6)]
    ln = LNScratch(p, "ln")

    xch = [p.sb(f"xch_{i}", [128, 256], F32) for i in range(8)]
    xn_bf = [p.sb(f"xnb_{i}", [128, T], BF16) for i in range(8)]
    xcy = [p.sb(f"xcy_{i}", [128, T], BF16) for i in range(16)]
    dtc_bf = p.sb("dtc_bf", [128, T], BF16)
    zld = dtc_bf
    sc_a = p.sb("sc_a", [128, T], BF16)
    sc_b = p.sb("sc_b", [128, T], BF16)
    sc_dh = p.sb("sc_dh", [128, T], BF16)
    Brep = p.sb("Brep", [128, T], BF16)
    Crep = p.sb("Crep", [128, T], BF16)
    f32A = p.sb("f32A", [128, T + 4], F32)
    f32B = p.sb("f32B", [128, T + 4], F32)
    f32C = p.sb("f32C", [128, T], F32)
    zbf = p.sb("zbf", [128, 512], BF16)
    dt_bf = p.sb("dt_bf", [128, 512], BF16)
    xe_bf = p.sb("xe_bf", [128, 512], BF16)
    Bsb_bf = p.sb("Bsb_bf", [16, T], BF16)
    Csb_bf = p.sb("Csb_bf", [16, T], BF16)
    dbc0_bf = p.sb("dbc0_bf", [1, T], BF16)

    wsin = WStream(p, "Win", w_in, DM, 2 * D_INNER, nbuf=1)
    wsB = WStream(p, "WpB", xproj_B, D_INNER, 16, nbuf=1)
    wsC = WStream(p, "WpC", xproj_C, D_INNER, 16, nbuf=1)
    wsdt = WStream(p, "Wpdt", xproj_dt, D_INNER, 1, nbuf=1)
    wsout = WStream(p, "Wout", w_out, D_INNER, DM, nbuf=1)

    for b in range(BPC):
        for ci, c0 in enumerate(range(0, T, 256)):
            for ti in range(8):
                p.dma("sp", xch[ti].ap, x_in[b, 128 * ti:128 * (ti + 1), c0:c0 + 256], w=[xch[ti].k])
            def ln_cb(ti, cc0, cw, xn_ap, xn_key, c0=c0):
                p.run("act", lambda e, ti=ti, a=xn_ap: e.copy(
                    out=xn_bf[ti].t[:, c0:c0 + 256], in_=a), r=[xn_key], w=[xn_bf[ti].k])
            fm_layernorm(p, C, ln, xch, 256, ln_cb, onec_f, onec_f, ones_f, psP[5],
                         gain=[pc(NW, i) for i in range(8)], bias=[pc(NB, i) for i in range(8)],
                         eps=1e-5, D_feat=1024)
        if dbg and b == 0:
            for ti in range(8):
                p.dma("sp", dbg_xn[ti], xn_bf[ti].ap, r=[xn_bf[ti].k])
        def win_cb(mi, m0, mw, c0, cw, psum):
            if mi < 16:
                p.run("act", lambda e, c0=c0, cw=cw, ps=psum: e.copy(
                    out=f32A.t[:, 4 + c0:4 + c0 + cw], in_=ps.t[:, 0:cw]), r=[psum.k], w=[f32A.k])
                if c0 + cw == T:
                    p.run("dve", lambda e: e.memset(f32A.t[:, 1:4], 0.0), r=[], w=[f32A.k])
                    if dbg and b == 0 and mi == 0:
                        p.dma("sp", dbg_xc0, f32A.ap, r=[f32A.k])
                    p.run("dve", lambda e, mi=mi: e.tensor_scalar(
                        out=f32B.t[:, 0:T], in0=f32A.t[:, 1:1 + T], scalar1=pc(CW, 4 * mi),
                        scalar2=pc(CB, mi), op0=ALU.mult, op1=ALU.add),
                        r=[f32A.k, "params"], w=[f32B.k])
                    for k in (1, 2, 3):
                        src, dst = (f32B, f32C) if k % 2 == 1 else (f32C, f32B)
                        p.run("dve", lambda e, mi=mi, k=k, src=src, dst=dst: e.scalar_tensor_tensor(
                            out=dst.t[:, 0:T], in0=f32A.t[:, 1 + k:1 + k + T], scalar=pc(CW, 4 * mi + k),
                            in1=src.t[:, 0:T], op0=ALU.mult, op1=ALU.add),
                            r=[f32A.k, "params", src.k], w=[dst.k])
                    if dbg and b == 0 and mi == 0:
                        p.dma("sp", dbg_cv0, f32C.t[:, 0:T], r=[f32C.k])
                    p.run("act", lambda e, mi=mi: e.activation(out=xcy[mi].ap, in_=f32C.t[:, 0:T], func=AF.Silu),
                          r=[f32C.k], w=[xcy[mi].k])
            else:
                p.run("act", lambda e, c0=c0, cw=cw, ps=psum: e.copy(
                    out=zbf.t[:, 0:cw], in_=ps.t[:, 0:cw]), r=[psum.k], w=[zbf.k])
                p.dma("sp", z_scr[mi - 16, :, c0:c0 + cw], zbf.t[:, 0:cw], r=[zbf.k])
        fm_matmul(p, wsin, xn_bf, T, win_cb, psP[0:2])
        if dbg and b == 0:
            for ti in range(16):
                p.dma("sp", dbg_xcs[ti], xcy[ti].ap, r=[xcy[ti].k])
        for ws, dst in ((wsB, Bsb_bf), (wsC, Csb_bf), (wsdt, None)):
            def xp_cb(mi, m0, mw, c0, cw, psum, dst=dst):
                if dst is None:
                    p.run("act", lambda e, c0=c0, cw=cw, ps=psum: e.copy(
                        out=dbc0_bf.t[0:1, c0:c0 + cw], in_=ps.t[0:1, 0:cw]), r=[psum.k], w=[dbc0_bf.k])
                else:
                    p.run("act", lambda e, c0=c0, cw=cw, ps=psum, mwid=mw: e.copy(
                        out=dst.t[0:mwid, c0:c0 + cw], in_=ps.t[0:mwid, 0:cw]), r=[psum.k], w=[dst.k])
            fm_matmul(p, ws, xcy, T, xp_cb, psP[2:4])
        if dbg and b == 0:
            p.dma("sp", dbg_B, Bsb_bf.ap, r=[Bsb_bf.k])
            p.dma("sp", dbg_dbc0, dbc0_bf.ap, r=[dbc0_bf.k])
        for c0 in range(0, T, 512):
            p.mm([lambda e, c0=c0: e.matmul(psP[4].t[:, :], sel16_sb.ap, Bsb_bf.t[:, c0:c0 + 512], start=True, stop=True, skip_group_check=True)],
                 r=[Bsb_bf.k, sel16_sb.k], w=[psP[4].k])
            p.run("dve", lambda e, c0=c0: e.tensor_copy(out=Brep.t[:, c0:c0 + 512], in_=psP[4].t[:, :]),
                  r=[psP[4].k], w=[Brep.k])
            p.mm([lambda e, c0=c0: e.matmul(psP[4].t[:, :], sel16_sb.ap, Csb_bf.t[:, c0:c0 + 512], start=True, stop=True, skip_group_check=True)],
                 r=[Csb_bf.k, sel16_sb.k], w=[psP[4].k])
            p.run("dve", lambda e, c0=c0: e.tensor_copy(out=Crep.t[:, c0:c0 + 512], in_=psP[4].t[:, :]),
                  r=[psP[4].k], w=[Crep.k])
            p.mm([lambda e, c0=c0: e.matmul(psP[4].t[:, :], ones_b.ap, dbc0_bf.t[0:1, c0:c0 + 512], start=True, stop=True, skip_group_check=True)],
                 r=[dbc0_bf.k, ones_b.k], w=[psP[4].k])
            p.run("dve", lambda e, c0=c0: e.tensor_copy(out=f32A.t[:, c0:c0 + 512], in_=psP[4].t[:, :]),
                  r=[psP[4].k], w=[f32A.k])
        for dtile in range(16):
            p.run("dve", lambda e, dtile=dtile: e.tensor_scalar(
                out=f32B.t[:, 0:T], in0=f32A.t[:, 0:T], scalar1=pc(DTW, dtile), scalar2=pc(DTB, dtile),
                op0=ALU.mult, op1=ALU.add), r=[f32A.k, "params"], w=[f32B.k])
            softplus(p, C, dtc_bf, f32B, f32C, f32B, T)
            if dbg and b == 0:
                p.dma("sp", dbg_dt[dtile], dtc_bf.ap, r=[dtc_bf.k])
            for j in range(16):
                st_idx = dtile * 16 + j
                selr = sel_sb.t[:, 128 * j:128 * (j + 1)]
                redr = red_sb.t[:, 128 * j:128 * (j + 1)]
                for ci, c0 in enumerate(range(0, T, 512)):
                    p.mm([lambda e, c0=c0, selr=selr: e.matmul(psP[4].t[:, :], selr, dtc_bf.t[:, c0:c0 + 512], start=True, stop=True, skip_group_check=True)],
                         r=[dtc_bf.k, sel_sb.k], w=[psP[4].k])
                    p.run("act", lambda e, c0=c0: e.copy(out=sc_dh.t[:, c0:c0 + 512], in_=psP[4].t[:, :]),
                          r=[psP[4].k], w=[sc_dh.k])
                    p.mm([lambda e, c0=c0, selr=selr, dtile=dtile: e.matmul(psP[5].t[:, :], selr, xcy[dtile].t[:, c0:c0 + 512], start=True, stop=True, skip_group_check=True)],
                         r=[xcy[dtile].k, sel_sb.k], w=[psP[5].k])
                    p.run("act", lambda e, c0=c0: e.copy(out=xe_bf.t[:, 0:512], in_=psP[5].t[:, :]),
                          r=[psP[5].k], w=[xe_bf.k])
                    p.run("dve", lambda e, c0=c0: e.tensor_tensor(
                        out=sc_b.t[:, c0:c0 + 512], in0=sc_dh.t[:, c0:c0 + 512], in1=Brep.t[:, c0:c0 + 512], op=ALU.mult),
                        r=[sc_dh.k, Brep.k], w=[sc_b.k])
                    p.run("dve", lambda e, c0=c0: e.tensor_tensor(
                        out=sc_b.t[:, c0:c0 + 512], in0=sc_b.t[:, c0:c0 + 512], in1=xe_bf.t[:, 0:512], op=ALU.mult),
                        r=[sc_b.k, xe_bf.k], w=[sc_b.k])
                p.run("act", lambda e, st=st_idx: e.activation(out=sc_a.ap, in_=sc_dh.ap, func=AF.Exp,
                      scale=napos.t[:, st:st + 1]), r=[sc_dh.k, napos.k], w=[sc_a.k])
                for ci, c0 in enumerate(range(0, T, 512)):
                    init = 0.0 if ci == 0 else sc_dh.t[:, c0 - 1:c0]
                    p.run("dve", lambda e, c0=c0, init=init: e.tensor_tensor_scan(
                        out=sc_dh.t[:, c0:c0 + 512], data0=sc_a.t[:, c0:c0 + 512], data1=sc_b.t[:, c0:c0 + 512],
                        initial=init, op0=ALU.mult, op1=ALU.add), r=[sc_a.k, sc_b.k, sc_dh.k], w=[sc_dh.k])
                if dbg and b == 0 and dtile == 0 and j == 0:
                    p.dma("sp", dbg_h0, sc_dh.ap, r=[sc_dh.k])
                p.run("dve", lambda e: e.tensor_tensor(out=sc_b.ap, in0=sc_dh.ap, in1=Crep.ap, op=ALU.mult),
                      r=[sc_dh.k, Crep.k], w=[sc_b.k])
                for ci, c0 in enumerate(range(0, T, 512)):
                    p.mm([lambda e, c0=c0, redr=redr, ci=ci, j=j: e.matmul(
                        psP[ci].t[:, :], redr, sc_b.t[:, c0:c0 + 512], start=(j == 0), stop=(j == 15), skip_group_check=True)],
                        r=[sc_b.k, red_sb.k], w=[psP[ci].k])
            p.dma("sp", zld.ap, z_scr[dtile, :, :], r=[], w=[zld.k])
            for ci, c0 in enumerate(range(0, T, 512)):
                p.run("act", lambda e, ci=ci: e.copy(out=ln.xf.t[:, 0:512], in_=psP[ci].t[:, :]),
                      r=[psP[ci].k], w=[ln.xf.k])
                p.run("act", lambda e, dtile=dtile, c0=c0: e.copy(out=ln.xn.t[:, 0:512], in_=xcy[dtile].t[:, c0:c0 + 512]),
                      r=[xcy[dtile].k], w=[ln.xn.k])
                p.run("dve", lambda e, dtile=dtile: e.scalar_tensor_tensor(
                    out=ln.xf.t[:, 0:512], in0=ln.xn.t[:, 0:512], scalar=pc(DV, dtile), in1=ln.xf.t[:, 0:512],
                    op0=ALU.mult, op1=ALU.add), r=[ln.xn.k, "params", ln.xf.k], w=[ln.xf.k])
                p.run("act", lambda e, c0=c0: e.activation(out=ln.xn.t[:, 0:512], in_=zld.t[:, c0:c0 + 512], func=AF.Silu),
                      r=[zld.k], w=[ln.xn.k])
                p.run("dve", lambda e, dtile=dtile, c0=c0: e.tensor_tensor(
                    out=xcy[dtile].t[:, c0:c0 + 512], in0=ln.xf.t[:, 0:512], in1=ln.xn.t[:, 0:512], op=ALU.mult),
                    r=[ln.xf.k, ln.xn.k], w=[xcy[dtile].k])
        if dbg and b == 0:
            for ti in range(16):
                p.dma("sp", dbg_y[ti], xcy[ti].ap, r=[xcy[ti].k])
        def wout_cb(mi, m0, mw, c0, cw, psum):
            p.dma("sp", ln.xf.ap[0:128, 0:cw] if False else ln.xf.t[:, 0:cw],
                  x_in[b, 128 * mi:128 * (mi + 1), c0:c0 + cw], w=[ln.xf.k])
            p.run("dve", lambda e, mi=mi, c0=c0, cw=cw, ps=psum: e.scalar_tensor_tensor(
                out=xn_bf[mi].t[:, c0:c0 + cw], in0=ln.xf.t[:, 0:cw], scalar=C[2.0], in1=ps.t[:, 0:cw],
                op0=ALU.mult, op1=ALU.add), r=[ln.xf.k, "consts", psum.k], w=[xn_bf[mi].k])
        fm_matmul(p, wsout, xcy, T, wout_cb, psP[4:6])
        def pln_cb(ti, c0, cw, xn_ap, xn_key):
            p.dma("sp", x1_out[b, 128 * ti:128 * (ti + 1), c0:c0 + cw], xn_ap, r=[xn_key])
        fm_layernorm(p, C, ln, xn_bf, T, pln_cb, onec_b, onec_f, ones_f, psP[3], eps=1e-5, D_feat=1024)
    return p, C


def p1_inputs(inp, core):
    x = inp["L0_attn_x"][2 * core:2 * core + 2]
    x_fm = np.ascontiguousarray(x.transpose(0, 2, 1))
    W_in = np.ascontiguousarray(inp["L0_attn_W_in"])
    xp = inp["L0_attn_x_proj_w"]
    A_log = inp["L0_attn_A_log"]
    st = np.arange(256)
    dp = np.arange(8)
    n = np.arange(16)
    d_idx = (8 * st[None, :, None] + dp[:, None, None])
    A_rep = A_log[8 * st[:, None, None] // 1, :]
    A_rep = np.empty((128, 256), np.float32)
    for s in range(256):
        for pp in range(128):
            A_rep[pp, s] = A_log[8 * s + pp // 16, pp % 16]
    prm = np.zeros((128, 144), np.float32)
    for i in range(8):
        prm[:, 0 + i] = inp["L0_attn_norm_w"][128 * i:128 * (i + 1)]
        prm[:, 8 + i] = inp["L0_attn_norm_b"][128 * i:128 * (i + 1)]
    for ti in range(16):
        sl = slice(128 * ti, 128 * (ti + 1))
        for k in range(4):
            prm[:, 16 + 4 * ti + k] = inp["L0_attn_conv_w"][sl, k]
        prm[:, 80 + ti] = inp["L0_attn_conv_b"][sl]
        prm[:, 96 + ti] = inp["L0_attn_dt_w"][sl]
        prm[:, 112 + ti] = inp["L0_attn_dt_b"][sl]
        prm[:, 128 + ti] = inp["L0_attn_D"][sl]
    sel = np.zeros((128, 16 * 128), np.float32)
    red = np.zeros((128, 16 * 128), np.float32)
    for r in range(16):
        for pp in range(128):
            k = 8 * r + pp // 16
            sel[k, 128 * r + pp] = 1.0
            red[pp, 128 * r + k] = 1.0
    sel16 = np.zeros((16, 128), np.float32)
    for pp in range(128):
        sel16[pp % 16, pp] = 1.0
    return {
        "x_fm": x_fm.astype(np.float32),
        "W_in": W_in.astype(np.float32),
        "xproj_B": np.ascontiguousarray(xp[:, 1:17]).astype(np.float32),
        "xproj_C": np.ascontiguousarray(xp[:, 17:33]).astype(np.float32),
        "xproj_dt": np.ascontiguousarray(xp[:, 0:1]).astype(np.float32),
        "A_log_rep": A_rep,
        "W_out": np.ascontiguousarray(inp["L0_attn_W_out"]).astype(np.float32),
        "sel_mats": sel.astype(BF),
        "red_mats": red.astype(BF),
        "sel16": sel16.astype(BF),
        "ones1x128f": np.ones((1, 128), np.float32),
        "ones1x128b": np.ones((1, 128), BF),
        "prm_tab": prm,
    }




INNER_PAD = 2816

def rope_chunk(p, C, q_dst, dst_col, src_ps, src_key, invn_bcast, scale_col, Pmat, CS, SS,
               tab_col, ps_sw, sc, name_sc):
    qb, sw, m1 = sc
    if invn_bcast is None:
        p.run("dve", lambda e: e.tensor_copy(out=qb.ap, in_=src_ps.t[:, 0:512]), r=[src_ps.k], w=[qb.k])
    else:
        p.run("dve", lambda e: e.tensor_tensor(out=qb.ap, in0=src_ps.t[:, 0:512], in1=invn_bcast.ap, op=ALU.mult),
              r=[src_ps.k, invn_bcast.k], w=[qb.k])
    if scale_col is not None:
        p.run("dve", lambda e: e.tensor_scalar(out=qb.ap, in0=qb.ap, scalar1=scale_col, scalar2=None, op0=ALU.mult),
              r=[qb.k, "params"], w=[qb.k])
    p.run("act", lambda e: e.copy(out=m1.ap, in_=qb.ap), r=[qb.k], w=[m1.k])
    p.mm([lambda e: e.matmul(ps_sw.t[:, 0:512], Pmat.ap, m1.ap, start=True, stop=True, skip_group_check=True)],
         r=[Pmat.k, m1.k], w=[ps_sw.k])
    p.run("act", lambda e: e.copy(out=sw.ap, in_=ps_sw.ap), r=[ps_sw.k], w=[sw.k])
    p.run("dve", lambda e: e.tensor_tensor(out=m1.ap, in0=m1.ap, in1=CS.t[:, tab_col:tab_col + 512], op=ALU.mult),
          r=[m1.k, CS.k], w=[m1.k])
    p.run("dve", lambda e: e.tensor_tensor(out=sw.ap, in0=sw.ap, in1=SS.t[:, tab_col:tab_col + 512], op=ALU.mult),
          r=[sw.k, SS.k], w=[sw.k])
    p.run("dve", lambda e: e.tensor_tensor(out=q_dst.t[:, dst_col:dst_col + 512], in0=m1.ap, in1=sw.ap, op=ALU.add),
          r=[m1.k, sw.k], w=[q_dst.k])


def build_p2():
    p = Prog("p2")
    C = Consts(p)
    x_in = p.dram_in("x_fm", [BPC, DM, T])
    wqkv = p.dram_in("Wqkv", [DM, 3 * 1024])
    wo = p.dram_in("Wo", [1024, DM])
    w1 = p.dram_in("W1p", [DM, 2 * INNER_PAD])
    w2 = p.dram_in("W2p", [INNER_PAD, DM])
    CSq_in = p.dram_in("CSq", [128, T], BF16)
    SSq_in = p.dram_in("SSq", [128, T], BF16)
    CSk_in = p.dram_in("CSk", [128, T], BF16)
    SSk_in = p.dram_in("SSk", [128, T], BF16)
    pmat_in = p.dram_in("Pmat", [128, 128], BF16)
    ident_in = p.dram_in("identb", [128, 128], BF16)
    m0_in = p.dram_in("mask0", [2, 128, 256], F32)
    m1_in = p.dram_in("mask1", [2, 128, 512], F32)
    ones_f_in = p.dram_in("ones1x128f", [1, 128], F32)
    prm_in = p.dram_in("prm_tab", [128, 72], F32)
    x_out = p.dram_out("xo_fm", [BPC, DM, T])
    x_mid = p.nc.dram_tensor("x_mid", [DM, T], F32, kind="Internal").ap()

    ones_f = p.sb("ones_f", [1, 128], F32)
    onec_f = p.sb("onec_f", [128, 1], F32)
    onec_b = p.sb("onec_b", [128, 1], BF16)
    p.dma("sp", ones_f.ap, ones_f_in, w=[ones_f.k])
    p.dma("sp", onec_f.ap, ones_f_in.rearrange("a b -> b a"), w=[onec_f.k])
    CSq = p.sb("tCSq", [128, T], BF16); SSq = p.sb("tSSq", [128, T], BF16)
    CSk = p.sb("tCSk", [128, T], BF16); SSk = p.sb("tSSk", [128, T], BF16)
    for t_, a_ in ((CSq, CSq_in), (SSq, SSq_in), (CSk, CSk_in), (SSk, SSk_in)):
        p.dma("sp", t_.ap, a_, w=[t_.k])
    Pmat = p.sb("tPmat", [128, 128], BF16)
    identb = p.sb("identb_sb", [128, 128], BF16)
    p.dma("sp", Pmat.ap, pmat_in, w=[Pmat.k])
    p.dma("sp", identb.ap, ident_in, w=[identb.k])
    p.run("pool", lambda e: e.tensor_copy(out=onec_b.ap, in_=onec_f.ap), r=[onec_f.k], w=[onec_b.k])
    mask0 = p.sb("tmask0", [128, 2 * 256], F32)
    mask1 = p.sb("tmask1", [128, 2 * 512], F32)
    p.dma("sp", mask0.t[:, 0:256], m0_in[0], w=[mask0.k])
    p.dma("sp", mask0.t[:, 256:512], m0_in[1], w=[mask0.k])
    p.dma("sp", mask1.t[:, 0:512], m1_in[0], w=[mask1.k])
    p.dma("sp", mask1.t[:, 512:1024], m1_in[1], w=[mask1.k])
    prm = p.sb("prm", [128, 72], F32)
    p.dma("sp", prm.ap, prm_in, w=["params"])
    def pc(i):
        return prm.t[:, i:i + 1]

    psP = [p.ps(f"psp{i}", [128, 512], F32) for i in range(5)]
    ps_tr = p.ps("ps_tr", [128, 512], BF16)
    ln = LNScratch(p, "ln")

    pool = [p.sb(f"pl_{i}", [128, T], BF16) for i in range(32)]
    xs = pool[0:8]
    qb_t = pool[8:16]
    kb_t = pool[16:24]
    vp = pool[24:32]
    abf = xs + kb_t + vp[0:6]
    xch = [p.sb(f"xch_{i}", [128, 256], F32) for i in range(8)]
    sqs = p.sb("sqs", [128, 512], F32)
    nrm1 = ln.var
    nrm2 = ln.nm
    invb = p.sb("invb", [128, 512], F32)
    probs = p.sb("probs", [128, 512], F32)
    probs_bf = p.sb("probs_bf", [128, 512], BF16)
    attnT = p.sb("attnT", [128, 512], BF16)
    rmax = p.sb("rmax", [128, 1], F32)
    rsum = p.sb("rsum", [128, 1], F32)
    rrec = p.sb("rrec", [128, 1], F32)
    qbc = p.sb("qbc", [128, 512], BF16)
    swc = p.sb("swc", [128, 512], BF16)
    m1c = p.sb("m1c", [128, 512], BF16)
    gch = p.sb("gch", [128, 512], BF16)
    resf = p.sb("resf", [128, 512], F32)

    wsqkv = WStream(p, "Wqkv", wqkv, DM, 3 * 1024, nbuf=1)
    wswo = WStream(p, "Wo", wo, 1024, DM, nbuf=1)
    wsw1 = WStream(p, "W1", w1, DM, 2 * INNER_PAD, nbuf=1)
    wsw2 = WStream(p, "W2", w2, INNER_PAD, DM, nbuf=1)

    for b in range(BPC):
        for ti in range(8):
            for c0 in range(0, T, 256):
                p.dma("sp", xch[ti % 8].ap, x_in[b, 128 * ti:128 * (ti + 1), c0:c0 + 256], w=[xch[ti % 8].k])
                if ti < 4:
                    p.run("act", lambda e, ti=ti, c0=c0: e.copy(out=xs[ti].t[:, c0:c0 + 256], in_=xch[ti % 8].ap),
                          r=[xch[ti % 8].k], w=[xs[ti].k])
                else:
                    if c0 == 0:
                        p.run("dve", lambda e, ti=ti: e.memset(xs[ti].t[:, 0:1], 0.0), r=[], w=[xs[ti].k])
                    p.run("act", lambda e, ti=ti, c0=c0: e.copy(
                        out=xs[ti].t[:, c0 + 1:min(c0 + 257, T)], in_=xch[ti % 8].t[:, 0:255 if c0 + 257 > T else 256]),
                        r=[xch[ti % 8].k], w=[xs[ti].k])
        for which, dst, CSx, SSx, scol, mbase in (("q", qb_t, CSq, SSq, pc(52), 0), ("k", kb_t, CSk, SSk, pc(53), 1024)):
            def qk_cb(mi, m0, mw, c0, cw, psum, dst=dst, CSx=CSx, SSx=SSx, scol=scol):
                p.run("act", lambda e, ps=psum: e.activation(out=sqs.ap, in_=ps.t[:, 0:512], func=AF.Square),
                      r=[psum.k], w=[sqs.k])
                p.mm([lambda e: e.matmul(ln.ps_s.t[0:1, 0:512], onec_f.t[0:64, 0:1], sqs.t[0:64, :], start=True, stop=True, skip_group_check=True)],
                     r=[sqs.k, onec_f.k], w=[ln.ps_s.k])
                p.mm([lambda e: e.matmul(ln.ps_q.t[0:1, 0:512], onec_f.t[64:128, 0:1], sqs.t[64:128, :], start=True, stop=True, skip_group_check=True)],
                     r=[sqs.k, onec_f.k], w=[ln.ps_q.k])
                for nrm, pss in ((nrm1, ln.ps_s), (nrm2, ln.ps_q)):
                    p.run("act", lambda e, nrm=nrm, pss=pss: e.activation(out=nrm.t[0:1, :], in_=pss.t[0:1, 0:512], func=AF.Sqrt),
                          r=[pss.k], w=[nrm.k])
                    p.run("dve", lambda e, nrm=nrm: e.tensor_scalar(out=nrm.t[0:1, :], in0=nrm.t[0:1, :], scalar1=C.p1(1e-12), scalar2=None, op0=ALU.max),
                          r=[nrm.k, "consts"], w=[nrm.k])
                    p.run("dve", lambda e, nrm=nrm: e.reciprocal(out=nrm.t[0:1, :], in_=nrm.t[0:1, :]),
                          r=[nrm.k], w=[nrm.k])
                p.mm([lambda e: e.matmul(psP[4].t[0:64, 0:512], ones_f.t[0:1, 0:64], nrm1.t[0:1, :], start=True, stop=True, skip_group_check=True)],
                     r=[nrm1.k, ones_f.k], w=[psP[4].k])
                p.mm([lambda e: e.matmul(psP[4].t[64:128, 0:512], ones_f.t[0:1, 0:64], nrm2.t[0:1, :], start=True, stop=True, skip_group_check=True)],
                     r=[nrm2.k, ones_f.k], w=[psP[4].k])
                p.run("dve", lambda e: e.tensor_copy(out=invb.ap, in_=psP[4].t[:, 0:512]), r=[psP[4].k], w=[invb.k])
                rope_chunk(p, C, dst[mi], c0, psum, psum.k, invb, scol, Pmat, CSx, SSx, c0,
                           psP[3], (qbc, swc, m1c), "rp")
            fm_matmul(p, wsqkv, xs, T, qk_cb, psP[0:2],
                      mtiles=[(mbase + 128 * i, 128) for i in range(8)])
        for m in range(8):
            wsqkv.load(2048 + 128 * m, 128, m)
            for ti in range(16):
                fns = []
                reads = []
                psum = psP[2 + (ti % 2)]
                for k in range(8):
                    wt = wsqkv.tile(m, k)
                    fns.append(lambda e, wt=wt, k=k, ti=ti, ps=psum, st=(k == 0), sp=(k == 7):
                               e.matmul(ps.t[:, 0:128], xs[k].t[:, 128 * ti:128 * (ti + 1)], wt.ap,
                                        start=st, stop=sp, skip_group_check=True))
                    reads.append(wt.k)
                    reads.append(xs[k].k)
                p.mm(fns, r=reads, w=[psum.k])
                p.run("act", lambda e, m=m, ti=ti, ps=psum: e.copy(
                    out=vp[ti // 2].t[:, 1024 * (ti % 2) + 128 * m:1024 * (ti % 2) + 128 * (m + 1)], in_=ps.t[:, 0:128]),
                    r=[psum.k], w=[vp[ti // 2].k])
        QKS = 8.0
        for h in range(16):
            qt = qb_t[h // 2]
            kt_ = kb_t[h // 2]
            hrow = 64 * (h % 2)
            for w_ in range(8):
                for ih in range(2):
                    i0 = 256 * w_ + 128 * ih
                    jw = 256 if w_ == 0 else 512
                    j0 = 0 if w_ == 0 else 256 * (w_ - 1)
                    msk = mask0 if w_ == 0 else mask1
                    mcol = ih * jw
                    sps = psP[0]
                    p.mm([lambda e, qt=qt, kt_=kt_, hrow=hrow, i0=i0, j0=j0, jw=jw, sps=sps:
                          e.matmul(sps.t[:, 0:jw], qt.t[hrow:hrow + 64, i0:i0 + 128], kt_.t[hrow:hrow + 64, j0:j0 + jw],
                                   start=True, stop=True, skip_group_check=True)],
                         r=[qt.k, kt_.k], w=[sps.k])
                    p.run("dve", lambda e, jw=jw, mcol=mcol, msk=msk, sps=sps: e.scalar_tensor_tensor(
                        out=probs.t[:, 0:jw], in0=sps.t[:, 0:jw], scalar=C[QKS], in1=msk.t[:, mcol:mcol + jw],
                        op0=ALU.mult, op1=ALU.add), r=[sps.k, "consts", msk.k], w=[probs.k])
                    p.run("dve", lambda e, jw=jw: e.tensor_reduce(out=rmax.ap, in_=probs.t[:, 0:jw], axis=AX.X, op=ALU.max, negate=True),
                          r=[probs.k], w=[rmax.k])
                    p.run("act", lambda e, jw=jw: e.activation(out=probs.t[:, 0:jw], in_=probs.t[:, 0:jw], func=AF.Exp,
                          bias=rmax.ap, accum_out=rsum.ap), r=[probs.k, rmax.k], w=[probs.k, rsum.k])
                    p.run("dve", lambda e: e.reciprocal(out=rrec.ap, in_=rsum.ap), r=[rsum.k], w=[rrec.k])
                    p.run("dve", lambda e, jw=jw: e.tensor_scalar(out=probs_bf.t[:, 0:jw], in0=probs.t[:, 0:jw],
                          scalar1=rrec.ap, scalar2=None, op0=ALU.mult), r=[probs.k, rrec.k], w=[probs_bf.k])
                    nj = jw // 128
                    p.mm([lambda e, jj=jj: e.transpose(ps_tr.t[:, 128 * jj:128 * (jj + 1)], probs_bf.t[:, 128 * jj:128 * (jj + 1)], identb.ap)
                          for jj in range(nj)], r=[probs_bf.k, identb.k], w=[ps_tr.k])
                    p.run("act", lambda e, jw=jw: e.copy(out=attnT.t[:, 0:jw], in_=ps_tr.t[:, 0:jw]),
                          r=[ps_tr.k], w=[attnT.k])
                    fns = []
                    reads = [attnT.k]
                    ops_ = psP[3]
                    for jj in range(nj):
                        jt = (j0 + 128 * jj) // 128
                        vtile = vp[jt // 2]
                        vcol = 1024 * (jt % 2) + 64 * h
                        fns.append(lambda e, vtile=vtile, vcol=vcol, jj=jj, ops_=ops_, st=(jj == 0), sp=(jj == nj - 1):
                                   e.matmul(ops_.t[0:64, 0:128], vtile.t[:, vcol:vcol + 64], attnT.t[:, 128 * jj:128 * (jj + 1)],
                                            start=st, stop=sp, skip_group_check=True))
                        reads.append(vtile.k)
                    p.mm(fns, r=reads, w=[ops_.k])
                    p.run("act", lambda e, h=h, hrow=hrow, i0=i0, ops_=ops_: e.copy(
                        out=xs[h // 2].t[hrow:hrow + 64, i0:i0 + 128], in_=ops_.t[0:64, 0:128]),
                        r=[ops_.k], w=[xs[h // 2].k])
        def wo_cb(mi, m0, mw, c0, cw, psum):
            p.dma("sp", resf.t[:, 0:cw], x_in[b, 128 * mi:128 * (mi + 1), c0:c0 + cw], w=[resf.k])
            p.run("dve", lambda e, c0=c0, cw=cw, ps=psum: e.tensor_tensor(
                out=resf.t[:, 0:cw], in0=resf.t[:, 0:cw], in1=ps.t[:, 0:cw], op=ALU.add),
                r=[resf.k, psum.k], w=[resf.k])
            p.dma("sp", x_mid[128 * mi:128 * (mi + 1), c0:c0 + cw], resf.t[:, 0:cw], r=[resf.k])
        fm_matmul(p, wswo, xs, T, wo_cb, psP[0:2])
        for c0 in range(0, T, 256):
            for ti in range(8):
                if ti < 4:
                    p.dma("sp", xch[ti].ap, x_mid[128 * ti:128 * (ti + 1), c0:c0 + 256], w=[xch[ti].k])
                else:
                    if c0 == 0:
                        p.run("dve", lambda e, ti=ti: e.memset(xch[ti].t[:, 0:1], 0.0), r=[], w=[xch[ti].k])
                        p.dma("sp", xch[ti].t[:, 1:256], x_mid[128 * ti:128 * (ti + 1), 0:255], w=[xch[ti].k])
                    else:
                        p.dma("sp", xch[ti].ap, x_mid[128 * ti:128 * (ti + 1), c0 - 1:c0 + 255], w=[xch[ti].k])
            def ln_cb(ti, cc0, cw, xn_ap, xn_key, c0=c0):
                p.run("act", lambda e, ti=ti, a=xn_ap: e.copy(
                    out=qb_t[ti].t[:, c0:c0 + 256], in_=a), r=[xn_key], w=[qb_t[ti].k])
            fm_layernorm(p, C, ln, xch, 256, ln_cb, onec_f, onec_f, ones_f, psP[4],
                         gain=[prm.t[:, 54 + i:55 + i] for i in range(8)],
                         bias=[prm.t[:, 62 + i:63 + i] for i in range(8)], eps=1e-5, D_feat=1024)
        def w1_cb(mi, m0, mw, c0, cw, psum):
            if mi < 22:
                p.run("dve", lambda e, mi=mi, c0=c0, cw=cw, ps=psum: e.tensor_scalar(
                    out=abf[mi].t[:, c0:c0 + cw], in0=ps.t[:, 0:cw], scalar1=pc(mi), scalar2=None, op0=ALU.add),
                    r=[psum.k, "params"], w=[abf[mi].k])
            else:
                p.run("act", lambda e, mi=mi, c0=c0, cw=cw, ps=psum: e.activation(
                    out=gch.t[:, 0:cw], in_=ps.t[:, 0:cw], func=AF.Gelu, bias=pc(mi)),
                    r=[psum.k, "params"], w=[gch.k])
                p.run("dve", lambda e, mi=mi, c0=c0, cw=cw: e.tensor_tensor(
                    out=abf[mi - 22].t[:, c0:c0 + cw], in0=abf[mi - 22].t[:, c0:c0 + cw], in1=gch.t[:, 0:cw], op=ALU.mult),
                    r=[abf[mi - 22].k, gch.k], w=[abf[mi - 22].k])
        fm_matmul(p, wsw1, qb_t, T, w1_cb, psP[0:2])
        def w2_cb(mi, m0, mw, c0, cw, psum):
            p.dma("sp", resf.t[:, 0:cw], x_mid[128 * mi:128 * (mi + 1), c0:c0 + cw], w=[resf.k])
            p.run("dve", lambda e, mi=mi, cw=cw, ps=psum: e.tensor_scalar(
                out=ln.xf.t[:, 0:cw], in0=ps.t[:, 0:cw], scalar1=pc(44 + mi), scalar2=None, op0=ALU.add),
                r=[psum.k, "params"], w=[ln.xf.k])
            p.run("dve", lambda e, cw=cw: e.tensor_tensor(
                out=resf.t[:, 0:cw], in0=resf.t[:, 0:cw], in1=ln.xf.t[:, 0:cw], op=ALU.add),
                r=[resf.k, ln.xf.k], w=[resf.k])
            p.dma("sp", x_out[b, 128 * mi:128 * (mi + 1), c0:c0 + cw], resf.t[:, 0:cw], r=[resf.k])
        fm_matmul(p, wsw2, abf, T, w2_cb, psP[2:4])
    return p, C


def p2_tables():
    half = np.arange(0, 64, 2, dtype=np.float64)
    inv_freq = 1.0 / 10000.0 ** (half / 64.0)
    t = np.arange(T, dtype=np.float64)
    freqs = t[:, None] * inv_freq[None, :]
    freqs_cat = np.concatenate([freqs, freqs], -1)
    base_scale = (half + 0.4 * 64) / (1.4 * 64)
    power = (t - 1024) / 128.0
    scale = base_scale[None, :] ** power[:, None]
    scale_cat = np.concatenate([scale, scale], -1)
    cosv = np.cos(freqs_cat)
    sinv = np.sin(freqs_cat)
    j = np.arange(128) % 64
    CSq = (cosv * scale_cat)[:, j].T
    SSq = (sinv * scale_cat)[:, j].T
    CSk = (cosv / scale_cat)[:, j].T
    SSk = (sinv / scale_cat)[:, j].T
    P = np.zeros((128, 128), np.float64)
    for pp in range(128):
        base = (pp // 64) * 64
        jj = pp % 64
        if jj < 32:
            P[base + jj + 32, pp] = -1.0
        else:
            P[base + jj - 32, pp] = 1.0
    NEG = -3.0e38
    m0 = np.full((2, 128, 256), NEG, np.float32)
    m1 = np.full((2, 128, 512), NEG, np.float32)
    for ih in range(2):
        for r in range(128):
            i_ = 128 * ih + r
            jsm = np.arange(256)
            m0[ih, r, (jsm <= i_)] = 0.0
            jsl = np.arange(512)
            m1[ih, r, (jsl >= i_) & (jsl <= i_ + 256)] = 0.0
    return (CSq.astype(BF), SSq.astype(BF), CSk.astype(BF), SSk.astype(BF),
            P.astype(BF), m0, m1)


def p2_inputs(inp, core, layer):
    x = None
    Wqkv = np.ascontiguousarray(inp["L0_ffn_Wqkv"][layer]).astype(np.float32)
    Wo = np.ascontiguousarray(inp["L0_ffn_Wo"][layer]).astype(np.float32)
    W1 = inp["L0_ffn_W1"][layer]
    W2 = inp["L0_ffn_W2"][layer]
    b1 = inp["L0_ffn_b1"][layer]
    b2 = inp["L0_ffn_b2"][layer]
    W1p = np.zeros((1024, 2 * INNER_PAD), np.float32)
    W1p[:, 0:2730] = W1[:, 0:2730]
    W1p[:, INNER_PAD:INNER_PAD + 2730] = W1[:, 2730:]
    W2p = np.zeros((INNER_PAD, 1024), np.float32)
    W2p[0:2730] = W2
    prm = np.zeros((128, 72), np.float32)
    b1a = np.zeros(INNER_PAD, np.float32); b1a[0:2730] = b1[0:2730]
    b1g = np.zeros(INNER_PAD, np.float32); b1g[0:2730] = b1[2730:]
    for i in range(22):
        prm[:, i] = b1a[128 * i:128 * (i + 1)]
        prm[:, 22 + i] = b1g[128 * i:128 * (i + 1)]
    for i in range(8):
        prm[:, 44 + i] = b2[128 * i:128 * (i + 1)]
        prm[:, 54 + i] = inp["L0_ffn_ln_g"][layer][128 * i:128 * (i + 1)]
        prm[:, 62 + i] = inp["L0_ffn_ln_b"][layer][128 * i:128 * (i + 1)]
    prm[:, 52] = np.tile(inp["L0_ffn_q_scale"][layer], 2)
    prm[:, 53] = np.tile(inp["L0_ffn_k_scale"][layer], 2)
    CSq, SSq, CSk, SSk, P, m0, m1 = p2_tables()
    return {
        "Wqkv": Wqkv, "Wo": Wo, "W1p": W1p, "W2p": W2p,
        "CSq": CSq, "SSq": SSq, "CSk": CSk, "SSk": SSk,
        "Pmat": P, "identb": np.eye(128).astype(BF),
        "mask0": m0, "mask1": m1,
        "ones1x128f": np.ones((1, 128), np.float32),
        "prm_tab": prm,
    }



def build_p3():
    p = Prog("p3")
    C = Consts(p)
    x1_in = p.dram_in("x1_fm", [BPC, DM, T])
    y6_in = p.dram_in("y6_fm", [BPC, DM, T])
    wq = p.dram_in("WqT", [DM, DM])
    wk = p.dram_in("WkT", [DM, DM])
    wv = p.dram_in("WvT", [DM, DM])
    wo = p.dram_in("WoT", [DM, DM])
    CS_in = p.dram_in("CSr", [128, T], BF16)
    SS_in = p.dram_in("SSr", [128, T], BF16)
    pmat_in = p.dram_in("Pmat", [128, 128], BF16)
    ident_in = p.dram_in("identb", [128, 128], BF16)
    ones_f_in = p.dram_in("ones1x128f", [1, 128], F32)
    x3_out = p.dram_out("x3_fm", [BPC, DM, T])
    x2_scr = p.nc.dram_tensor("x2_scr", [DM, T], F32, kind="Internal").ap()

    ones_f = p.sb("ones_f", [1, 128], F32)
    onec_f = p.sb("onec_f", [128, 1], F32)
    onec_b = p.sb("onec_b", [128, 1], BF16)
    p.dma("sp", ones_f.ap, ones_f_in, w=[ones_f.k])
    p.dma("sp", onec_f.ap, ones_f_in.rearrange("a b -> b a"), w=[onec_f.k])
    p.run("pool", lambda e: e.tensor_copy(out=onec_b.ap, in_=onec_f.ap), r=[onec_f.k], w=[onec_b.k])
    CS = p.sb("tCS", [128, T], BF16)
    SS = p.sb("tSS", [128, T], BF16)
    p.dma("sp", CS.ap, CS_in, w=[CS.k])
    p.dma("sp", SS.ap, SS_in, w=[SS.k])
    Pmat = p.sb("tPmat", [128, 128], BF16)
    identb = p.sb("identb_sb", [128, 128], BF16)
    p.dma("sp", Pmat.ap, pmat_in, w=[Pmat.k])
    p.dma("sp", identb.ap, ident_in, w=[identb.k])

    psP = [p.ps(f"psp{i}", [128, 512], F32) for i in range(5)]
    ps_tr = p.ps("ps_tr", [128, 512], BF16)
    ln = LNScratch(p, "ln")

    pool = [p.sb(f"pl_{i}", [128, T], BF16) for i in range(32)]
    x2b = pool[0:8]
    qb_t = pool[8:16]
    kb_t = pool[16:24]
    vp = pool[24:32]
    xch = [p.sb(f"xch_{i}", [128, 256], F32) for i in range(8)]
    ych = p.sb("ych", [128, 256], F32)
    probs_f = p.sb("probs_f", [128, T], F32)
    probs_bf = p.sb("probs_bf", [128, T], BF16)
    attnT = p.sb("attnT", [128, T], BF16)
    rmax = p.sb("rmax", [128, 1], F32)
    rsum = p.sb("rsum", [128, 1], F32)
    rrec = p.sb("rrec", [128, 1], F32)
    qbc = p.sb("qbc", [128, 512], BF16)
    swc = p.sb("swc", [128, 512], BF16)
    m1c = p.sb("m1c", [128, 512], BF16)
    resf = p.sb("resf", [128, 512], F32)

    wsq = WStream(p, "Wq", wq, DM, DM, nbuf=1)
    wsk = WStream(p, "Wk", wk, DM, DM, nbuf=1)
    wsv = WStream(p, "Wv", wv, DM, DM, nbuf=1)
    wso = WStream(p, "Wo", wo, DM, DM, nbuf=1)

    for b in range(BPC):
        for c0 in range(0, T, 256):
            for ti in range(8):
                p.dma("sp", xch[ti].ap, x1_in[b, 128 * ti:128 * (ti + 1), c0:c0 + 256], w=[xch[ti].k])
                p.dma("sp", ych.ap, y6_in[b, 128 * ti:128 * (ti + 1), c0:c0 + 256], w=[ych.k])
                p.run("dve", lambda e, ti=ti: e.tensor_tensor(out=xch[ti].ap, in0=xch[ti].ap, in1=ych.ap, op=ALU.add),
                      r=[xch[ti].k, ych.k], w=[xch[ti].k])
            def ln_cb(ti, cc0, cw, xn_ap, xn_key, c0=c0):
                p.run("act", lambda e, ti=ti, a=xn_ap: e.copy(
                    out=x2b[ti].t[:, c0:c0 + 256], in_=a), r=[xn_key], w=[x2b[ti].k])
                p.dma("sp", x2_scr[128 * ti:128 * (ti + 1), c0:c0 + 256], xn_ap, r=[xn_key])
            fm_layernorm(p, C, ln, xch, 256, ln_cb, onec_f, onec_f, ones_f, psP[4], eps=1e-5, D_feat=1024)
        for ws, dst in ((wsq, qb_t), (wsk, kb_t)):
            def qk_cb(mi, m0, mw, c0, cw, psum, dst=dst):
                rope_chunk(p, C, dst[mi], c0, psum, psum.k, None, None, Pmat, CS, SS, c0,
                           psP[3], (qbc, swc, m1c), "rp")
            fm_matmul(p, ws, x2b, T, qk_cb, psP[0:2])
        for m in range(8):
            wsv.load(128 * m, 128, m)
            for ti in range(16):
                fns = []
                reads = []
                psum = psP[2 + (ti % 2)]
                for k in range(8):
                    wt = wsv.tile(m, k)
                    fns.append(lambda e, wt=wt, k=k, ti=ti, ps=psum, st=(k == 0), sp=(k == 7):
                               e.matmul(ps.t[:, 0:128], x2b[k].t[:, 128 * ti:128 * (ti + 1)], wt.ap,
                                        start=st, stop=sp, skip_group_check=True))
                    reads.append(wt.k)
                    reads.append(x2b[k].k)
                p.mm(fns, r=reads, w=[psum.k])
                p.run("act", lambda e, m=m, ti=ti, ps=psum: e.copy(
                    out=vp[ti // 2].t[:, 1024 * (ti % 2) + 128 * m:1024 * (ti % 2) + 128 * (m + 1)], in_=ps.t[:, 0:128]),
                    r=[psum.k], w=[vp[ti // 2].k])
        ISC = 0.125
        for h in range(16):
            qt = qb_t[h // 2]
            kt_ = kb_t[h // 2]
            hrow = 64 * (h % 2)
            for ic in range(16):
                i0 = 128 * ic
                for jc4 in range(4):
                    sps = psP[jc4]
                    p.mm([lambda e, qt=qt, kt_=kt_, hrow=hrow, i0=i0, jc4=jc4, sps=sps:
                          e.matmul(sps.t[:, 0:512], qt.t[hrow:hrow + 64, i0:i0 + 128], kt_.t[hrow:hrow + 64, 512 * jc4:512 * (jc4 + 1)],
                                   start=True, stop=True, skip_group_check=True)],
                         r=[qt.k, kt_.k], w=[sps.k])
                    p.run("act", lambda e, jc4=jc4, sps=sps: e.activation(
                        out=probs_f.t[:, 512 * jc4:512 * (jc4 + 1)], in_=sps.t[:, 0:512], func=AF.Copy,
                        scale=C[ISC]), r=[sps.k, "consts"], w=[probs_f.k])
                p.run("dve", lambda e: e.tensor_reduce(out=rmax.ap, in_=probs_f.ap, axis=AX.X, op=ALU.max, negate=True),
                      r=[probs_f.k], w=[rmax.k])
                p.run("act", lambda e: e.activation(out=probs_f.ap, in_=probs_f.ap, func=AF.Exp,
                      bias=rmax.ap, accum_out=rsum.ap), r=[probs_f.k, rmax.k], w=[probs_f.k, rsum.k])
                p.run("dve", lambda e: e.reciprocal(out=rrec.ap, in_=rsum.ap), r=[rsum.k], w=[rrec.k])
                p.run("dve", lambda e: e.tensor_scalar(out=probs_bf.ap, in0=probs_f.ap,
                      scalar1=rrec.ap, scalar2=None, op0=ALU.mult), r=[probs_f.k, rrec.k], w=[probs_bf.k])
                for jg in range(4):
                    p.mm([lambda e, jj=jj, jg=jg: e.transpose(ps_tr.t[:, 128 * jj:128 * (jj + 1)],
                          probs_bf.t[:, 512 * jg + 128 * jj:512 * jg + 128 * (jj + 1)], identb.ap)
                          for jj in range(4)], r=[probs_bf.k, identb.k], w=[ps_tr.k])
                    p.run("act", lambda e, jg=jg: e.copy(out=attnT.t[:, 512 * jg:512 * (jg + 1)], in_=ps_tr.t[:, 0:512]),
                          r=[ps_tr.k], w=[attnT.k])
                fns = []
                reads = [attnT.k]
                ops_ = psP[4]
                for jt in range(16):
                    vtile = vp[jt // 2]
                    vcol = 1024 * (jt % 2) + 64 * h
                    fns.append(lambda e, vtile=vtile, vcol=vcol, jt=jt, ops_=ops_, st=(jt == 0), sp=(jt == 15):
                               e.matmul(ops_.t[0:64, 0:128], vtile.t[:, vcol:vcol + 64], attnT.t[:, 128 * jt:128 * (jt + 1)],
                                        start=st, stop=sp, skip_group_check=True))
                    reads.append(vtile.k)
                p.mm(fns, r=reads, w=[ops_.k])
                p.run("act", lambda e, h=h, hrow=hrow, i0=i0, ops_=ops_: e.copy(
                    out=x2b[h // 2].t[hrow:hrow + 64, i0:i0 + 128], in_=ops_.t[0:64, 0:128]),
                    r=[ops_.k], w=[x2b[h // 2].k])
        def wo_cb(mi, m0, mw, c0, cw, psum):
            p.dma("sp", resf.t[:, 0:cw], x2_scr[128 * mi:128 * (mi + 1), c0:c0 + cw], w=[resf.k])
            p.run("dve", lambda e, mi=mi, c0=c0, cw=cw, ps=psum: e.tensor_tensor(
                out=qb_t[mi].t[:, c0:c0 + cw], in0=resf.t[:, 0:cw], in1=ps.t[:, 0:cw], op=ALU.add),
                r=[resf.k, psum.k], w=[qb_t[mi].k])
        fm_matmul(p, wso, x2b, T, wo_cb, psP[0:2])
        def pln_cb(ti, c0, cw, xn_ap, xn_key):
            p.dma("sp", x3_out[b, 128 * ti:128 * (ti + 1), c0:c0 + cw], xn_ap, r=[xn_key])
        fm_layernorm(p, C, ln, qb_t, T, pln_cb, onec_b, onec_f, ones_f, psP[3], eps=1e-5, D_feat=1024)
    return p, C


def p3_inputs(inp):
    pos = np.arange(T, dtype=np.float64)[:, None]
    freqs = np.exp(-math.log(10000.0) * np.arange(0, 64, 2, dtype=np.float64) / 64)
    ang = pos * freqs
    sinv = np.sin(ang)
    cosv = np.cos(ang)
    m = (np.arange(128) % 64) // 2
    CS = cosv[:, m].T
    SS = sinv[:, m].T
    P = np.zeros((128, 128), np.float64)
    for pp in range(128):
        if pp % 2 == 0:
            P[pp + 1, pp] = -1.0
        else:
            P[pp - 1, pp] = 1.0
    return {
        "WqT": np.ascontiguousarray(inp["L1_attn_W_q"].T).astype(np.float32),
        "WkT": np.ascontiguousarray(inp["L1_attn_W_k"].T).astype(np.float32),
        "WvT": np.ascontiguousarray(inp["L1_attn_W_v"].T).astype(np.float32),
        "WoT": np.ascontiguousarray(inp["L1_attn_W_o"].T).astype(np.float32),
        "CSr": CS.astype(BF), "SSr": SS.astype(BF),
        "Pmat": P.astype(BF), "identb": np.eye(128).astype(BF),
        "ones1x128f": np.ones((1, 128), np.float32),
    }



def build_p4():
    p = Prog("p4")
    C = Consts(p)
    x3_in = p.dram_in("x3_fm", [BPC, DM, T])
    wgc_in = p.dram_in("wgc", [DM, 10])
    ones_f_in = p.dram_in("ones1x128f", [1, 128], F32)
    prm_in = p.dram_in("prm_tab", [128, 16], F32)
    ao_out = p.dram_out("ao_fm", [BPC, DM, T])
    mask_out = p.dram_out("mask1", [BPC * T, 8], BF16)
    gatec_out = p.dram_out("gatec", [BPC * T, 3], F32)
    cnt_out = p.dram_out("counts", [1, 8], F32)

    ones_f = p.sb("ones_f", [1, 128], F32)
    onec_f = p.sb("onec_f", [128, 1], F32)
    onec_b = p.sb("onec_b", [128, 1], BF16)
    p.dma("sp", ones_f.ap, ones_f_in, w=[ones_f.k])
    p.dma("sp", onec_f.ap, ones_f_in.rearrange("a b -> b a"), w=[onec_f.k])
    p.run("pool", lambda e: e.tensor_copy(out=onec_b.ap, in_=onec_f.ap), r=[onec_f.k], w=[onec_b.k])
    prm = p.sb("prm", [128, 16], F32)
    p.dma("sp", prm.ap, prm_in, w=["params"])
    wgc = [p.sb(f"wgc_{i}", [128, 10], F32) for i in range(8)]
    for i in range(8):
        p.dma("sp", wgc[i].ap, wgc_in[128 * i:128 * (i + 1), :], w=[wgc[i].k])

    psP = [p.ps(f"psp{i}", [128, 512], F32) for i in range(4)]
    ps_cnt = p.ps("ps_cnt", [1, 8], F32)
    ln = LNScratch(p, "ln")
    xch = [p.sb(f"xch_{i}", [128, 256], F32) for i in range(8)]
    ao_f = [p.sb(f"aof_{i}", [128, T], F32) for i in range(8)]
    lg = p.sb("lg", [128, 10], F32)
    mx = p.sb("mx", [128, 1], F32)
    msk = p.sb("msk", [128, 8], F32)
    mskb = p.sb("mskb", [128, 8], BF16)
    gsum = p.sb("gsum", [128, 1], F32)
    gout = p.sb("gout", [128, 3], F32)
    c01 = p.sb("c01", [128, 2], F32)
    csum = p.sb("csum", [128, 1], F32)

    for b in range(BPC):
        for c0 in range(0, T, 256):
            for ti in range(8):
                p.dma("sp", xch[ti].ap, x3_in[b, 128 * ti:128 * (ti + 1), c0:c0 + 256], w=[xch[ti].k])
            def ln_cb(ti, cc0, cw, xn_ap, xn_key, c0=c0):
                p.run("act", lambda e, ti=ti, a=xn_ap: e.copy(
                    out=ao_f[ti].t[:, c0:c0 + 256], in_=a), r=[xn_key], w=[ao_f[ti].k])
                p.dma("sp", ao_out[b, 128 * ti:128 * (ti + 1), c0:c0 + 256], xn_ap, r=[xn_key])
            fm_layernorm(p, C, ln, xch, 256, ln_cb, onec_f, onec_f, ones_f, psP[3],
                         gain=[prm.t[:, i:i + 1] for i in range(8)],
                         bias=[prm.t[:, 8 + i:9 + i] for i in range(8)],
                         eps=1e-12, D_feat=1024)
        for ci in range(16):
            t0 = 128 * ci
            fns = []
            reads = []
            psum = psP[ci % 2]
            for k in range(8):
                fns.append(lambda e, k=k, t0=t0, ps=psum, st=(k == 0), sp=(k == 7):
                           e.matmul(ps.t[:, 0:10], ao_f[k].t[:, t0:t0 + 128], wgc[k].ap,
                                    start=st, stop=sp, skip_group_check=True))
                reads.append(ao_f[k].k)
                reads.append(wgc[k].k)
            p.mm(fns, r=reads, w=[psum.k])
            p.run("act", lambda e, ps=psum: e.copy(out=lg.ap, in_=ps.t[:, 0:10]), r=[psum.k], w=[lg.k])
            p.run("dve", lambda e: e.tensor_reduce(out=mx.ap, in_=lg.t[:, 0:8], axis=AX.X, op=ALU.max),
                  r=[lg.k], w=[mx.k])
            p.run("dve", lambda e: e.tensor_scalar(out=msk.ap, in0=lg.t[:, 0:8], scalar1=mx.ap, scalar2=None, op0=ALU.is_ge),
                  r=[lg.k, mx.k], w=[msk.k])
            p.run("pool", lambda e: e.tensor_copy(out=mskb.ap, in_=msk.ap), r=[msk.k], w=[mskb.k])
            p.dma("sp", mask_out[b * T + t0:b * T + t0 + 128, :], mskb.ap, r=[mskb.k])
            p.mm([lambda e, ci=ci, b=b: e.matmul(ps_cnt.t[0:1, 0:8], onec_b.ap, mskb.ap,
                  start=(b == 0 and ci == 0), stop=(b == 1 and ci == 15), skip_group_check=True)],
                 r=[mskb.k, onec_b.k], w=[ps_cnt.k])
            p.run("dve", lambda e: e.tensor_scalar(out=mx.ap, in0=mx.ap, scalar1=C[-1.0], scalar2=None, op0=ALU.mult),
                  r=[mx.k, "consts"], w=[mx.k])
            p.run("act", lambda e: e.activation(out=msk.ap, in_=lg.t[:, 0:8], func=AF.Exp, bias=mx.ap,
                  accum_out=gsum.ap), r=[lg.k, mx.k], w=[msk.k, gsum.k])
            p.run("dve", lambda e: e.reciprocal(out=gout.t[:, 0:1], in_=gsum.ap), r=[gsum.k], w=[gout.k])
            p.run("dve", lambda e: e.tensor_reduce(out=mx.ap, in_=lg.t[:, 8:10], axis=AX.X, op=ALU.max, negate=True),
                  r=[lg.k], w=[mx.k])
            p.run("act", lambda e: e.activation(out=c01.ap, in_=lg.t[:, 8:10], func=AF.Exp, bias=mx.ap,
                  accum_out=csum.ap), r=[lg.k, mx.k], w=[c01.k, csum.k])
            p.run("dve", lambda e: e.reciprocal(out=csum.ap, in_=csum.ap), r=[csum.k], w=[csum.k])
            p.run("dve", lambda e: e.tensor_scalar(out=gout.t[:, 1:3], in0=c01.ap, scalar1=csum.ap, scalar2=None, op0=ALU.mult),
                  r=[c01.k, csum.k], w=[gout.k])
            p.dma("sp", gatec_out[b * T + t0:b * T + t0 + 128, :], gout.ap, r=[gout.k])
    p.run("act", lambda e: e.copy(out=gout.t[0:1, 0:3] if False else lg.t[0:1, 0:8], in_=ps_cnt.t[0:1, 0:8]),
          r=[ps_cnt.k], w=[lg.k])
    p.dma("sp", cnt_out, lg.t[0:1, 0:8], r=[lg.k])
    return p, C


def p4_inputs(inp):
    wgc = np.concatenate([inp["L1_ffn_wg"], inp["L1_ffn_res_coef"]], axis=1).astype(np.float32)
    prm = np.zeros((128, 16), np.float32)
    for i in range(8):
        prm[:, i] = inp["L1_ffn_attn_nw"][128 * i:128 * (i + 1)]
        prm[:, 8 + i] = inp["L1_ffn_attn_nb"][128 * i:128 * (i + 1)]
    return {"wgc": wgc, "ones1x128f": np.ones((1, 128), np.float32), "prm_tab": prm}



def build_p5(dbg=False):
    p = Prog("p5")
    C = Consts(p)
    S = BPC * T
    x3_in = p.dram_in("x3_fm", [BPC, DM, T])
    ao_in = p.dram_in("ao_fm", [BPC, DM, T])
    mrows_in = p.dram_in("mask_rows", [8, S], BF16)
    grows_in = p.dram_in("gate_rows", [3, S], F32)
    counts_in = p.dram_in("counts_all", [8, 8], F32)
    prefsel_in = p.dram_in("pref_sel", [8, 1], BF16)
    interw_in = p.dram_in("inter_cat", [DM, 8 * 4096])
    outw_in = p.dram_in("outw_cat", [8 * 4096, DM])
    riw_in = p.dram_in("res_inter_w", [DM, 4096])
    row_in = p.dram_in("res_outw", [4096, DM])
    ones_f_in = p.dram_in("ones1x128f", [1, 128], F32)
    sel8_in = p.dram_in("sel8", [8, 8 * 128], BF16)
    prm_in = p.dram_in("prm_tab", [128, 400], F32)
    out_fm = p.dram_out("out_fm", [BPC, DM, T])
    if dbg:
        dbg_wf = p.dram_out("dbg_wf", [8, S], BF16)
        dbg_macc = p.dram_out("dbg_macc", [8, 128, T], BF16)
        dbg_racc = p.dram_out("dbg_racc", [8, 128, T], BF16)
        dbg_c1b = p.dram_out("dbg_c1b", [128, T], BF16)
        dbg_h = p.dram_out("dbg_h", [8, 128, T], BF16)
        dbg_wbc = p.dram_out("dbg_wbc", [128, T], BF16)

    B1, B2, RIB, ROB = 0, 256, 320, 352
    ones_f = p.sb("ones_f", [1, 128], F32)
    onec_f = p.sb("onec_f", [128, 1], F32)
    onec_b = p.sb("onec_b", [128, 1], BF16)
    p.dma("sp", ones_f.ap, ones_f_in, w=[ones_f.k])
    p.dma("sp", onec_f.ap, ones_f_in.rearrange("a b -> b a"), w=[onec_f.k])
    p.run("pool", lambda e: e.tensor_copy(out=onec_b.ap, in_=onec_f.ap), r=[onec_f.k], w=[onec_b.k])
    prm = p.sb("prm", [128, 400], F32)
    p.dma("sp", prm.ap, prm_in, w=["params"])
    sel8 = p.sb("sel8_sb", [8, 8 * 128], BF16)
    p.dma("sp", sel8.ap, sel8_in, w=[sel8.k])
    mrows = p.sb("mrows", [8, S], BF16)
    p.dma("sp", mrows.ap, mrows_in, w=[mrows.k])
    cnts = p.sb("cnts", [8, 8], BF16)
    cnts_f = p.sb("cnts_f", [8, 8], F32)
    p.dma("sp", cnts_f.ap, counts_in, w=[cnts_f.k])
    p.run("pool", lambda e: e.tensor_copy(out=cnts.ap, in_=cnts_f.ap), r=[cnts_f.k], w=[cnts.k])
    prefsel = p.sb("prefsel", [8, 1], BF16)
    p.dma("sp", prefsel.ap, prefsel_in, w=[prefsel.k])

    psP = [p.ps(f"psp{i}", [128, 512], F32) for i in range(5)]
    ps_off = p.ps("ps_off", [8, 8], F32)
    ln = LNScratch(p, "ln")

    onesr = p.sb("onesr", [8, 512], BF16)
    p.run("dve", lambda e: e.memset(onesr.ap, 1.0), r=[], w=[onesr.k])
    locs = p.sb("locs", [8, 512], F32)
    carry = p.sb("carry", [8, 1], F32)
    wf01 = p.sb("wf01", [8, S], BF16)
    offc = p.sb("offc", [8, 1], F32)
    prefsel_f = p.sb("prefsel_f", [8, 1], F32)
    p.run("act", lambda e: e.copy(out=prefsel_f.ap, in_=prefsel.ap), r=[prefsel.k], w=[prefsel_f.k])
    p.mm([lambda e: e.matmul(ps_off.t[0:8, 0:1], cnts_f.ap, prefsel_f.ap, start=True, stop=True, skip_group_check=True)],
         r=[cnts_f.k, prefsel_f.k], w=[ps_off.k])
    p.run("act", lambda e: e.copy(out=offc.ap, in_=ps_off.t[0:8, 0:1]), r=[ps_off.k], w=[offc.k])
    p.run("dve", lambda e: e.memset(carry.ap, 0.0), r=[], w=[carry.k])
    for ci, c0 in enumerate(range(0, S, 512)):
        init = 0.0 if ci == 0 else carry.ap
        p.run("dve", lambda e, c0=c0, init=init: e.tensor_tensor_scan(
            out=locs.ap, data0=onesr.ap, data1=mrows.t[0:8, c0:c0 + 512],
            initial=init, op0=ALU.mult, op1=ALU.add), r=[onesr.k, mrows.k, carry.k], w=[locs.k])
        p.run("act", lambda e: e.copy(out=carry.ap, in_=locs.t[0:8, 511:512]), r=[locs.k], w=[carry.k])
        p.run("dve", lambda e, c0=c0: e.tensor_tensor(out=locs.ap, in0=locs.ap, in1=mrows.t[0:8, c0:c0 + 512], op=ALU.subtract),
              r=[locs.k, mrows.k], w=[locs.k])
        p.run("dve", lambda e: e.tensor_scalar(out=locs.ap, in0=locs.ap, scalar1=offc.ap, scalar2=None, op0=ALU.add),
              r=[locs.k, offc.k], w=[locs.k])
        p.run("dve", lambda e, c0=c0: e.tensor_scalar(out=wf01.t[0:8, c0:c0 + 512], in0=locs.ap,
              scalar1=C.pn(4096.0, 8), scalar2=None, op0=ALU.is_lt), r=[locs.k, "consts"], w=[wf01.k])
        p.run("dve", lambda e, c0=c0: e.tensor_tensor(out=wf01.t[0:8, c0:c0 + 512], in0=wf01.t[0:8, c0:c0 + 512],
              in1=mrows.t[0:8, c0:c0 + 512], op=ALU.mult), r=[wf01.k, mrows.k], w=[wf01.k])

    if dbg:
        p.dma("sp", dbg_wf, wf01.ap, r=[wf01.k])
    ao_bf = [p.sb(f"aob_{i}", [128, T], BF16) for i in range(8)]
    htil = [p.sb(f"h_{i}", [128, T], BF16) for i in range(8)]
    macc = [p.sb(f"macc_{i}", [128, T], BF16) for i in range(8)]
    racc = [p.sb(f"racc_{i}", [128, T], BF16) for i in range(8)]
    hraw = p.sb("hraw", [128, T], BF16)
    g1 = p.sb("g1", [128, T], BF16)
    g2 = p.sb("g2", [128, T], BF16)
    xch0 = p.sb("xch0", [128, 256], F32)
    c0b = p.sb("c0b", [128, T], BF16)
    c1b = p.sb("c1b", [128, T], BF16)
    wbc = p.sb("wbc", [128, T], BF16)
    rowch = p.sb("rowch", [1, 512], F32)
    rowc2 = p.sb("rowc2", [1, 512], F32)

    ws_i = WStream(p, "Wi", interw_in, DM, 8 * 4096, nbuf=1)
    ws_o = WStream(p, "Wo2", outw_in, 1024, DM, nbuf=1)
    ws_ri = WStream(p, "Wri", riw_in, DM, 4096, nbuf=1)
    ws_ro = WStream(p, "Wro", row_in, 1024, DM, nbuf=1)

    def gelu_tanh_tile(x, out, t1, t2):
        p.run("act", lambda e: e.activation(out=t1.ap, in_=x.ap, func=AF.Square), r=[x.k], w=[t1.k])
        p.run("dve", lambda e: e.tensor_tensor(out=t1.ap, in0=t1.ap, in1=x.ap, op=ALU.mult), r=[t1.k, x.k], w=[t1.k])
        p.run("dve", lambda e: e.scalar_tensor_tensor(out=t2.ap, in0=t1.ap, scalar=C[0.044715],
              in1=x.ap, op0=ALU.mult, op1=ALU.add), r=[t1.k, x.k, "consts"], w=[t2.k])
        p.run("act", lambda e: e.activation(out=t1.ap, in_=t2.ap, func=AF.Tanh,
              scale=C[0.7978845608028654]), r=[t2.k, "consts"], w=[t1.k])
        p.run("dve", lambda e: e.scalar_tensor_tensor(out=t2.ap, in0=t1.ap, scalar=C[1.0],
              in1=x.ap, op0=ALU.add, op1=ALU.mult), r=[t1.k, x.k, "consts"], w=[t2.k])
        p.run("act", lambda e: e.activation(out=out.ap, in_=t2.ap, func=AF.Copy, scale=C[0.5]),
              r=[t2.k, "consts"], w=[out.k])

    for tc in range(BPC):
        for ti in range(8):
            for c0 in range(0, T, 256):
                p.dma("sp", xch0.ap, ao_in[tc, 128 * ti:128 * (ti + 1), c0:c0 + 256], w=[xch0.k])
                p.run("pool", lambda e, ti=ti, c0=c0: e.tensor_copy(out=ao_bf[ti].t[:, c0:c0 + 256], in_=xch0.ap),
                      r=[xch0.k], w=[ao_bf[ti].k])
        for c0 in range(0, T, 512):
            p.dma("sp", rowch.ap, grows_in[1:2, tc * T + c0:tc * T + c0 + 512], w=[rowch.k])
            p.mm([lambda e: e.matmul(psP[4].t[:, 0:512], ones_f.ap, rowch.ap, start=True, stop=True, skip_group_check=True)],
                 r=[rowch.k, ones_f.k], w=[psP[4].k])
            p.run("act", lambda e, c0=c0: e.copy(out=c0b.t[:, c0:c0 + 512], in_=psP[4].t[:, 0:512]),
                  r=[psP[4].k], w=[c0b.k])
            p.dma("sp", rowch.ap, grows_in[2:3, tc * T + c0:tc * T + c0 + 512], w=[rowch.k])
            p.dma("sp", rowc2.ap, grows_in[0:1, tc * T + c0:tc * T + c0 + 512], w=[rowc2.k])
            p.run("dve", lambda e: e.tensor_tensor(out=rowch.ap, in0=rowch.ap, in1=rowc2.ap, op=ALU.mult),
                  r=[rowch.k, rowc2.k], w=[rowch.k])
            p.mm([lambda e: e.matmul(psP[4].t[:, 0:512], ones_f.ap, rowch.ap, start=True, stop=True, skip_group_check=True)],
                 r=[rowch.k, ones_f.k], w=[psP[4].k])
            p.run("act", lambda e, c0=c0: e.copy(out=c1b.t[:, c0:c0 + 512], in_=psP[4].t[:, 0:512]),
                  r=[psP[4].k], w=[c1b.k])
        for mi in range(8):
            p.run("dve", lambda e, mi=mi: e.memset(macc[mi].ap, 0.0), r=[], w=[macc[mi].k])
            p.run("dve", lambda e, mi=mi: e.memset(racc[mi].ap, 0.0), r=[], w=[racc[mi].k])
        for e_ in range(8):
            for c0 in range(0, T, 512):
                p.mm([lambda e, c0=c0, e_=e_, tc=tc: e.matmul(psP[4].t[:, 0:512], sel8.t[:, 128 * e_:128 * (e_ + 1)],
                      wf01.t[0:8, tc * T + c0:tc * T + c0 + 512], start=True, stop=True, skip_group_check=True)],
                     r=[sel8.k, wf01.k], w=[psP[4].k])
                p.run("act", lambda e, c0=c0: e.copy(out=wbc.t[:, c0:c0 + 512], in_=psP[4].t[:, 0:512]),
                      r=[psP[4].k], w=[wbc.k])
            for q in range(4):
                def w1_cb(mi, m0, mw, c0, cw, psum, e_=e_, q=q):
                    p.run("dve", lambda e, mi=mi, c0=c0, cw=cw, ps=psum: e.tensor_scalar(
                        out=hraw.t[:, c0:c0 + cw], in0=ps.t[:, 0:cw],
                        scalar1=prm.t[:, B1 + 32 * e_ + 8 * q + mi:B1 + 32 * e_ + 8 * q + mi + 1],
                        scalar2=None, op0=ALU.add), r=[psum.k, "params"], w=[hraw.k])
                    if c0 + cw == T:
                        gelu_tanh_tile(hraw, htil[mi], g1, g2)
                        p.run("dve", lambda e, mi=mi: e.tensor_tensor(
                            out=htil[mi].ap, in0=htil[mi].ap, in1=wbc.ap, op=ALU.mult),
                            r=[htil[mi].k, wbc.k], w=[htil[mi].k])
                fm_matmul(p, ws_i, ao_bf, T, w1_cb, psP[0:2],
                          mtiles=[(4096 * e_ + 1024 * q + 128 * i, 128) for i in range(8)])
                def w2_cb(mi, m0, mw, c0, cw, psum, e_=e_, q=q):
                    p.run("act", lambda e, c0=c0, cw=cw, ps=psum: e.copy(out=g1.t[:, c0:c0 + cw], in_=ps.t[:, 0:cw]),
                          r=[psum.k], w=[g1.k])
                    p.run("dve", lambda e, mi=mi, c0=c0, cw=cw: e.tensor_tensor(
                        out=macc[mi].t[:, c0:c0 + cw], in0=macc[mi].t[:, c0:c0 + cw], in1=g1.t[:, c0:c0 + cw], op=ALU.add),
                        r=[macc[mi].k, g1.k], w=[macc[mi].k])
                    if q == 0:
                        p.run("dve", lambda e, mi=mi, c0=c0, cw=cw, e_=e_: e.scalar_tensor_tensor(
                            out=macc[mi].t[:, c0:c0 + cw], in0=wbc.t[:, c0:c0 + cw],
                            scalar=prm.t[:, B2 + 8 * e_ + mi:B2 + 8 * e_ + mi + 1],
                            in1=macc[mi].t[:, c0:c0 + cw], op0=ALU.mult, op1=ALU.add),
                            r=[wbc.k, "params", macc[mi].k], w=[macc[mi].k])
                if dbg and tc == 0 and e_ == 2 and q == 0:
                    for i_ in range(8):
                        p.dma("sp", dbg_h[i_], htil[i_].ap, r=[htil[i_].k])
                    p.dma("sp", dbg_wbc, wbc.ap, r=[wbc.k])
                fm_matmul(p, ws_o, htil, T, w2_cb, psP[2:4], kbase=4096 * e_ + 1024 * q)
        for q in range(4):
            def rw1_cb(mi, m0, mw, c0, cw, psum, q=q):
                p.run("dve", lambda e, mi=mi, c0=c0, cw=cw, ps=psum: e.tensor_scalar(
                    out=hraw.t[:, c0:c0 + cw], in0=ps.t[:, 0:cw],
                    scalar1=prm.t[:, RIB + 8 * q + mi:RIB + 8 * q + mi + 1],
                    scalar2=None, op0=ALU.add), r=[psum.k, "params"], w=[hraw.k])
                if c0 + cw == T:
                    gelu_tanh_tile(hraw, htil[mi], g1, g2)
            fm_matmul(p, ws_ri, ao_bf, T, rw1_cb, psP[0:2],
                      mtiles=[(1024 * q + 128 * i, 128) for i in range(8)])
            def rw2_cb(mi, m0, mw, c0, cw, psum, q=q):
                p.run("act", lambda e, c0=c0, cw=cw, ps=psum: e.copy(out=g1.t[:, c0:c0 + cw], in_=ps.t[:, 0:cw]),
                      r=[psum.k], w=[g1.k])
                p.run("dve", lambda e, mi=mi, c0=c0, cw=cw: e.tensor_tensor(
                    out=racc[mi].t[:, c0:c0 + cw], in0=racc[mi].t[:, c0:c0 + cw], in1=g1.t[:, c0:c0 + cw], op=ALU.add),
                    r=[racc[mi].k, g1.k], w=[racc[mi].k])
            fm_matmul(p, ws_ro, htil, T, rw2_cb, psP[2:4], kbase=1024 * q)
        if dbg and tc == 0:
            for mi in range(8):
                p.dma("sp", dbg_macc[mi], macc[mi].ap, r=[macc[mi].k])
                p.dma("sp", dbg_racc[mi], racc[mi].ap, r=[racc[mi].k])
            p.dma("sp", dbg_c1b, c1b.ap, r=[c1b.k])
        for mi in range(8):
            p.run("dve", lambda e, mi=mi: e.tensor_scalar(
                out=racc[mi].ap, in0=racc[mi].ap, scalar1=prm.t[:, ROB + mi:ROB + mi + 1], scalar2=None, op0=ALU.add),
                r=[racc[mi].k, "params"], w=[racc[mi].k])
            p.run("dve", lambda e, mi=mi: e.tensor_tensor(out=racc[mi].ap, in0=racc[mi].ap, in1=c0b.ap, op=ALU.mult),
                  r=[racc[mi].k, c0b.k], w=[racc[mi].k])
            p.run("dve", lambda e, mi=mi: e.tensor_tensor(out=macc[mi].ap, in0=macc[mi].ap, in1=c1b.ap, op=ALU.mult),
                  r=[macc[mi].k, c1b.k], w=[macc[mi].k])
            p.run("dve", lambda e, mi=mi: e.tensor_tensor(out=racc[mi].ap, in0=racc[mi].ap, in1=macc[mi].ap, op=ALU.add),
                  r=[racc[mi].k, macc[mi].k], w=[racc[mi].k])
            for c0 in range(0, T, 256):
                p.dma("sp", xch0.ap, x3_in[tc, 128 * mi:128 * (mi + 1), c0:c0 + 256], w=[xch0.k])
                p.run("pool", lambda e, c0=c0: e.tensor_copy(out=g1.t[:, c0:c0 + 256], in_=xch0.ap),
                      r=[xch0.k], w=[g1.k])
                p.run("dve", lambda e, mi=mi, c0=c0: e.scalar_tensor_tensor(
                    out=racc[mi].t[:, c0:c0 + 256], in0=g1.t[:, c0:c0 + 256], scalar=C[2.0],
                    in1=racc[mi].t[:, c0:c0 + 256], op0=ALU.mult, op1=ALU.add),
                    r=[g1.k, "consts", racc[mi].k], w=[racc[mi].k])
        def fln_cb(ti, c0, cw, xn_ap, xn_key):
            p.dma("sp", out_fm[tc, 128 * ti:128 * (ti + 1), c0:c0 + cw], xn_ap, r=[xn_key])
        fm_layernorm(p, C, ln, racc, T, fln_cb, onec_b, onec_f, ones_f, psP[4], eps=1e-5, D_feat=1024)
    return p, C


def p5_inputs(inp, core):
    iw = inp["L1_ffn_inter_w"]
    ow = inp["L1_ffn_output_w"]
    inter_cat = np.ascontiguousarray(np.concatenate([iw[e] for e in range(8)], axis=1)).astype(np.float32)
    outw_cat = np.ascontiguousarray(np.concatenate([ow[e] for e in range(8)], axis=0)).astype(np.float32)
    prm = np.zeros((128, 400), np.float32)
    for e in range(8):
        for i in range(32):
            prm[:, 0 + 32 * e + i] = inp["L1_ffn_inter_b"][e][128 * i:128 * (i + 1)]
        for i in range(8):
            prm[:, 256 + 8 * e + i] = inp["L1_ffn_output_b"][e][128 * i:128 * (i + 1)]
    for i in range(32):
        prm[:, 320 + i] = inp["L1_ffn_res_inter_b"][128 * i:128 * (i + 1)]
    for i in range(8):
        prm[:, 352 + i] = inp["L1_ffn_res_output_b"][128 * i:128 * (i + 1)]
    sel8 = np.zeros((8, 8 * 128), np.float32)
    for e in range(8):
        sel8[e, 128 * e:128 * (e + 1)] = 1.0
    pref = np.zeros((8, 1), np.float32)
    pref[:core, 0] = 1.0
    return {
        "inter_cat": inter_cat, "outw_cat": outw_cat,
        "res_inter_w": np.ascontiguousarray(inp["L1_ffn_res_inter_w"]).astype(np.float32),
        "res_outw": np.ascontiguousarray(inp["L1_ffn_res_output_w"]).astype(np.float32),
        "ones1x128f": np.ones((1, 128), np.float32),
        "sel8": sel8.astype(BF),
        "pref_sel": pref.astype(BF),
        "prm_tab": prm,
    }



_CACHE = {}

def _get(name, builder):
    if name not in _CACHE:
        p, C = builder()
        nc = p.finalize()
        _CACHE[name] = (nc, C.array())
    return _CACHE[name]


def kernel(**inputs):
    cores = list(range(NCORES))
    nc1, cst1 = _get("p1", build_p1)
    nc2, cst2 = _get("p2", build_p2)
    nc3, cst3 = _get("p3", build_p3)
    nc4, cst4 = _get("p4", build_p4)
    nc5, cst5 = _get("p5", build_p5)

    ims = []
    shared1 = None
    for c in cores:
        im = p1_inputs(inputs, c)
        im["consts_in"] = cst1
        ims.append(im)
    r1 = run_bass_kernel_spmd(nc1, ims, core_ids=cores)
    x1 = [np.asarray(r1.results[c]["x1_fm"], np.float32) for c in cores]

    cur = x1
    for layer in range(6):
        base = p2_inputs(inputs, 0, layer)
        base["consts_in"] = cst2
        ims = []
        for c in cores:
            im = dict(base)
            im["x_fm"] = cur[c]
            ims.append(im)
        r2 = run_bass_kernel_spmd(nc2, ims, core_ids=cores)
        cur = [np.asarray(r2.results[c]["xo_fm"], np.float32) for c in cores]

    base = p3_inputs(inputs)
    base["consts_in"] = cst3
    ims = []
    for c in cores:
        im = dict(base)
        im["x1_fm"] = x1[c]
        im["y6_fm"] = cur[c]
        ims.append(im)
    r3 = run_bass_kernel_spmd(nc3, ims, core_ids=cores)
    x3 = [np.asarray(r3.results[c]["x3_fm"], np.float32) for c in cores]

    base = p4_inputs(inputs)
    base["consts_in"] = cst4
    ims = []
    for c in cores:
        im = dict(base)
        im["x3_fm"] = x3[c]
        ims.append(im)
    r4 = run_bass_kernel_spmd(nc4, ims, core_ids=cores)
    ao = [np.asarray(r4.results[c]["ao_fm"], np.float32) for c in cores]
    masks = [np.asarray(r4.results[c]["mask1"]) for c in cores]
    gatecs = [np.asarray(r4.results[c]["gatec"], np.float32) for c in cores]
    counts_all = np.concatenate([np.asarray(r4.results[c]["counts"], np.float32) for c in cores], axis=0)

    ims = []
    for c in cores:
        im = p5_inputs(inputs, c)
        im["consts_in"] = cst5
        im["x3_fm"] = x3[c]
        im["ao_fm"] = ao[c]
        im["mask_rows"] = np.ascontiguousarray(masks[c].T)
        im["gate_rows"] = np.ascontiguousarray(gatecs[c].T)
        im["counts_all"] = counts_all
        ims.append(im)
    r5 = run_bass_kernel_spmd(nc5, ims, core_ids=cores)
    out = np.concatenate([np.asarray(r5.results[c]["out_fm"], np.float32) for c in cores], axis=0)
    return np.ascontiguousarray(out.transpose(0, 2, 1))
